# version 12
# baseline (speedup 1.0000x reference)
import numpy as np
import concourse.bass as bass
import concourse.mybir as mybir
from concourse.bass_utils import run_bass_kernel_spmd

F32 = mybir.dt.float32
BF16 = mybir.dt.bfloat16
AF = mybir.ActivationFunctionType
ALU = mybir.AluOpType
AX = mybir.AxisListType

D = 1024
JD = 8
E = 8
F = 4096
JF = 32
LN_EPS = 1e-5
NEG_BIG = -1.0e30


class Op:
    __slots__ = ("eng", "fn", "waits", "tok")
    def __init__(self, eng, fn, waits, tok):
        self.eng = eng; self.fn = fn; self.waits = waits; self.tok = tok

class P:
    ENGS = ("sync", "gpsimd", "vector", "scalar", "tensor")
    DMA_SEMS = frozenset({"init", "sx", "sx2", "gwp", "gw1a", "gw1b", "gw2a",
                          "gw2b", "cr1a", "cr1b", "cr2a", "cr2b", "gwo",
                          "gwc", "out"})

    def __init__(self, sems):
        self.sems = sems
        self.cnt = {k: 0 for k in sems}
        self.ops = []
        self.tap_gate = None

    def op(self, eng, fn=None, waits=(), inc=None, amt=None):
        w = [x for x in waits if x is not None]
        if self.tap_gate is not None:
            w.append(self.tap_gate)
        w = tuple((s, self.cnt[s]) if s in self.DMA_SEMS else (s, v) for (s, v) in w)
        tok = None
        if inc is not None:
            a = amt if amt is not None else 1
            self.cnt[inc] += a
            tok = (inc, self.cnt[inc], a)
        self.ops.append(Op(eng, fn, w, tok))
        return None if tok is None else (tok[0], tok[1])

    def dma(self, eng, fn, waits=(), inc=None):
        return self.op(eng, fn, waits=waits, inc=(inc or eng), amt=16)

    def emit(self, block):
        for eng in self.ENGS:
            getattr(block, eng)(self._mk_emitter(eng))

    def _mk_emitter(self, eng):
        def run(e):
            waited = {}
            for o in self.ops:
                if o.eng != eng:
                    continue
                for (s, v) in o.waits:
                    if waited.get(s, -1) >= v:
                        continue
                    e.wait_ge(self.sems[s], v)
                    waited[s] = v
                if o.fn is None:
                    continue
                inst = o.fn(e)
                if o.tok is not None:
                    inst.then_inc(self.sems[o.tok[0]], o.tok[2])
        return run



def build(T=4096, TQ=512, C=128, taps=()):
    taps = set(taps)
    NQ = T // TQ
    NCH = TQ // C
    NT = TQ // 128
    nc = bass.Bass("TRN2", target_bir_lowering=False)

    xH = nc.dram_tensor("xH", [128, JD, T], F32, kind="ExternalInput")
    wkH = nc.dram_tensor("wkH", [128, JD, D], F32, kind="ExternalInput")
    wvH = nc.dram_tensor("wvH", [128, JD, D], F32, kind="ExternalInput")
    wrH = nc.dram_tensor("wrH", [128, JD, D], F32, kind="ExternalInput")
    woH = nc.dram_tensor("woH", [128, JD, D], F32, kind="ExternalInput")
    w1H = nc.dram_tensor("w1H", [E, 8, 128, JD, 512], F32, kind="ExternalInput")
    w2H = nc.dram_tensor("w2H", [E, 8, 128, JF, 128], F32, kind="ExternalInput")
    vecsH = nc.dram_tensor("vecsH", [128, 8, JD], F32, kind="ExternalInput")
    b1H = nc.dram_tensor("b1H", [128, E, JF], F32, kind="ExternalInput")
    b2H = nc.dram_tensor("b2H", [128, E, E], F32, kind="ExternalInput")
    wqH = nc.dram_tensor("wqH", [128, JD, E], F32, kind="ExternalInput")
    brH = nc.dram_tensor("brH", [E, 1], F32, kind="ExternalInput")
    id128H = nc.dram_tensor("id128H", [128, 128], F32, kind="ExternalInput")
    onesH = nc.dram_tensor("onesH", [128, 1], F32, kind="ExternalInput")
    onesrowH = nc.dram_tensor("onesrowH", [1, 128], F32, kind="ExternalInput")
    selH = nc.dram_tensor("selH", [E, E, 128], F32, kind="ExternalInput")
    outH = nc.dram_tensor("outH", [128, JD, T], F32, kind="ExternalOutput")
    w1C = nc.dram_tensor("w1C", [E, 8, 128, JD, 512], BF16)
    w2C = nc.dram_tensor("w2C", [E, 8, 128, JF, 128], BF16)
    woC = nc.dram_tensor("woC", [128, JD, D], BF16)

    tap_t = {}
    tap_done = set()
    def mktap(name, shape, dtype=F32):
        if name in taps:
            tap_t[name] = nc.dram_tensor("tap_" + name, shape, dtype, kind="ExternalOutput")

    mktap("key_bf", [128, JD, C], BF16); mktap("val_bf", [128, JD, C], BF16)
    mktap("rcp_bf", [128, JD, C], BF16)
    mktap("s", [128, JD, C]); mktap("m", [128, JD, C])
    mktap("Lpos", [128, JD, C]); mktap("W", [128, JD, C]); mktap("Z", [128, JD, C])
    mktap("delta", [128, JD, C]); mktap("ew", [128, JD, C])
    mktap("n", [128, JD, C]); mktap("d", [128, JD, C]); mktap("wkv", [128, JD, C])
    mktap("z0", [128, JD, C]); mktap("x1", [128, JD, TQ])
    mktap("logits", [E, TQ]); mktap("rw", [E, TQ])
    mktap("h0", [128, JF, TQ], BF16); mktap("tgt2", [128, JD, TQ])

    oH = 0
    oW1 = 8192
    oW2 = 12288
    MOE_END = 14336
    oSA = 14336
    oSB = oSA + JD * C
    oSC = oSB + JD * C
    oSZ = oSC + JD * C
    oSV = oSZ + JD * (C + 1)
    oKB = oSV + JD * C
    oVB = oKB + JD * C // 2
    oRB = oVB + JD * C // 2
    oWKVR = oRB + JD * C // 2
    ARENA = oWKVR + NCH * (JD * C // 2)

    from contextlib import ExitStack
    with ExitStack() as stk:
        en = stk.enter_context
        x1 = en(nc.sbuf_tensor("x1", [128, JD, TQ], F32))
        x1bf = en(nc.sbuf_tensor("x1bf", [128, JD, TQ], BF16))
        tgt2 = en(nc.sbuf_tensor("tgt2", [128, JD, TQ], F32))
        xch = en(nc.sbuf_tensor("xch", [128, JD, C + 1], F32))
        wproj = en(nc.sbuf_tensor("wproj", [128, 3 * JD, D], BF16))
        vecs = en(nc.sbuf_tensor("vecs", [128, 8, JD], F32))
        dvec = en(nc.sbuf_tensor("dvec", [128, 3, JD], F32))
        b1s = en(nc.sbuf_tensor("b1s", [128, E, JF], F32))
        b2s = en(nc.sbuf_tensor("b2s", [128, E, E], F32))
        wqs = en(nc.sbuf_tensor("wqs", [128, JD, E], F32))
        brs = en(nc.sbuf_tensor("brs", [E, 1], F32))
        id128 = en(nc.sbuf_tensor("id128", [128, 128], F32))
        onescol = en(nc.sbuf_tensor("onescol", [128, 1], F32))
        onescolb = en(nc.sbuf_tensor("onescolb", [128, 1], BF16))
        onesrow = en(nc.sbuf_tensor("onesrow", [1, 128], F32))
        sels = en(nc.sbuf_tensor("sels", [E, E, 128], F32))
        carry = en(nc.sbuf_tensor("carry", [128, 5, JD], F32))
        bctr = en(nc.sbuf_tensor("bctr", [128, 2, E], F32))
        rwbc = en(nc.sbuf_tensor("rwbc", [128, TQ], F32))
        otmp = en(nc.sbuf_tensor("otmp", [128, TQ], F32))
        logits_sb = en(nc.sbuf_tensor("logits", [E, TQ], F32))
        rwfm = en(nc.sbuf_tensor("rwfm", [E, TQ], F32))
        lgT = en(nc.sbuf_tensor("lgT", [128, NT, E], F32))
        rows = en(nc.sbuf_tensor("rows", [1, 2 * TQ], F32))
        srow = en(nc.sbuf_tensor("srow", [128, 8], F32))
        epsc = en(nc.sbuf_tensor("epsc", [128, 1], F32))
        arena = en(nc.sbuf_tensor("arena", [128, ARENA], F32))
        psA = en(nc.psum_tensor("psA", [128, 512], F32))
        psB = en(nc.psum_tensor("psB", [128, 512], F32))
        psC = en(nc.psum_tensor("psC", [128, 512], F32))
        psD = en(nc.psum_tensor("psD", [128, 512], F32))
        psE = en(nc.psum_tensor("psE", [128, 512], F32))
        psF = en(nc.psum_tensor("psF", [128, 512], F32))
        rps = en(nc.psum_tensor("rps", [128, JD, C], F32))
        S = {}
        for nm in ["sync", "gpsimd", "vector", "scalar", "tensor", "out",
                   "gwp", "gw1a", "gw1b", "gw2a", "gw2b", "init", "sx", "sx2",
                   "gwc", "cr1a", "cr1b", "cr2a", "cr2b", "gwo"]:
            S[nm] = en(nc.semaphore("sm_" + nm))
        blk = en(nc.Block())

        p = P(S)

        def carve(off, *shape, dt=F32):
            n = int(np.prod(shape))
            nel = (n + 1) // 2 if dt == BF16 else n
            ap = arena.ap()[:, off:off + nel]
            if dt == BF16:
                ap = ap.bitcast(BF16)
            if len(shape) > 1:
                dims = " ".join(f"d{i}" for i in range(len(shape)))
                kw = {f"d{i}": s_ for i, s_ in enumerate(shape)}
                ap = ap.rearrange(f"p ({dims}) -> p {dims}", **kw)
            return ap

        hsb = carve(oH, JF, TQ, dt=BF16)
        wo_bf = carve(oH, JD, D, dt=BF16)
        w1s = [carve(oW1 + i * 2048, JD, 512, dt=BF16) for i in range(2)]
        w2s = [carve(oW2 + i * 1024, 16, 128, dt=BF16) for i in range(2)]
        xch2 = carve(oW2, JD, C)
        sA = carve(oSA, JD, C); sB = carve(oSB, JD, C); sC = carve(oSC, JD, C)
        sZ = carve(oSZ, JD, C + 1); sV = carve(oSV, JD, C)
        kbf = carve(oKB, JD, C, dt=BF16)
        vbf = carve(oVB, JD, C, dt=BF16)
        rbf = carve(oRB, JD, C, dt=BF16)
        wkvr4 = [carve(oWKVR + i * (JD * C // 2), JD, C, dt=BF16) for i in range(NCH)]

        wpv = wproj.ap().rearrange("p (m j) d -> p m j d", m=3)
        wk_bf, wv_bf, wr_bf = wpv[:, 0], wpv[:, 1], wpv[:, 2]

        V_U, V_W, V_TMK, V_TMV, V_TMR, V_BO, V_GAM, V_BET = range(8)
        D_UPW, D_UMW, D_2W = range(3)

        def j2(ap3, j, lo=None, hi=None):
            v = ap3[:, j:j + 1, :] if lo is None else ap3[:, j:j + 1, lo:hi]
            return v.rearrange("p a c -> p (a c)")

        def vcol(vi, j):
            return vecs.ap()[:, vi:vi + 1, j:j + 1].rearrange("p a b -> p (a b)")

        def dcol(di, j):
            return dvec.ap()[:, di:di + 1, j:j + 1].rearrange("p a b -> p (a b)")

        def ccol(i, j):
            return carry.ap()[:, i:i + 1, j:j + 1].rearrange("p a b -> p (a b)")

        def crow(i):
            return carry.ap()[:, i:i + 1, :].transpose([0, 2, 1])

        def tapout(name, ap, wait):
            if name in tap_t and name not in tap_done:
                tap_done.add(name)
                tt = tap_t[name]
                w = list(wait) if isinstance(wait, list) else [wait]
                tok = p.dma("sync", lambda e, ap=ap, tt=tt: e.dma_start(tt.ap(), ap),
                            waits=w, inc="out")
                p.tap_gate = tok
                return tok
            return None

        tL = {}
        tL["vecs"] = p.dma("sync", lambda e: e.dma_start(vecs[:], vecsH.ap()), inc="init")
        tL["b1"] = p.dma("sync", lambda e: e.dma_start(b1s[:], b1H.ap()), inc="init")
        tL["b2"] = p.dma("sync", lambda e: e.dma_start(b2s[:], b2H.ap()), inc="init")
        tL["wq"] = p.dma("sync", lambda e: e.dma_start(wqs[:], wqH.ap()), inc="init")
        tL["br"] = p.dma("sync", lambda e: e.dma_start(brs[:], brH.ap()), inc="init")
        tL["id"] = p.dma("sync", lambda e: e.dma_start(id128[:], id128H.ap()), inc="init")
        tL["on"] = p.dma("sync", lambda e: e.dma_start(onescol[:], onesH.ap()), inc="init")
        tL["or"] = p.dma("sync", lambda e: e.dma_start(onesrow[:], onesrowH.ap()), inc="init")
        tL["sel"] = p.dma("sync", lambda e: e.dma_start(sels[:], selH.ap()), inc="init")
        t_wp = [
            p.dma("gpsimd", lambda e: e.dma_start(wk_bf, wkH.ap()), inc="gwp"),
            p.dma("gpsimd", lambda e: e.dma_start(wv_bf, wvH.ap()), inc="gwp"),
            p.dma("gpsimd", lambda e: e.dma_start(wr_bf, wrH.ap()), inc="gwp"),
            p.dma("gpsimd", lambda e: e.dma_start(woC.ap(), woH.ap()), inc="gwo"),
        ]
        t_init = p.op("vector", lambda e: e.memset(carry[:], 0.0), inc="vector")
        t_eps = p.op("vector", lambda e: e.memset(epsc[:], LN_EPS), inc="vector")
        p.op("vector", lambda e: e.tensor_add(dvec[:, D_UPW], vecs[:, V_U], vecs[:, V_W]),
             waits=[tL["vecs"]], inc="vector")
        p.op("vector", lambda e: e.tensor_sub(dvec[:, D_UMW], vecs[:, V_U], vecs[:, V_W]),
             inc="vector")
        t_dvec = p.op("vector", lambda e: e.tensor_scalar_mul(dvec[:, D_2W], vecs[:, V_W], 2.0),
                      inc="vector")
        t_onesb = p.op("vector", lambda e: e.tensor_copy(onescolb[:], onescol[:]),
                       waits=[tL["on"]], inc="vector")

        bank = {"A": psA, "B": psB, "C": psC, "D": psD, "E": psE, "F": psF}
        blast = {}
        st = {
            "xch": None,
            "x1_free": [],
            "x1bf_free": [],
            "tgt2_free": None,
            "rps_free": None,
            "slots_free": [],
            "moe_done": [],
            "seg_x1": {},
            "wo_loaded": None,
            "tail_last_x1b": None,
            "ln_rw_done": None,
        }
        w1_war = [[], []]
        w2_war = [[], []]
        cache_done = [None]
        nw1 = [0]; nw2 = [0]
        seg_state = {}
        pending = []

        def seg(q, c):
            g = seg_gen(q, c)
            for _ in g:
                pass

        def pump(n=2):
            k = 0
            while pending and k < n:
                try:
                    next(pending[0])
                    k += 1
                except StopIteration:
                    pending.pop(0)

        def seg_gen(q, c):
            t0 = q * TQ + c * C
            first = (t0 == 0)
            gate = list(st["slots_free"])
            t_x = p.dma("sync", lambda e, t0=t0: e.dma_start(
                xch[:, :, 1:C + 1], xH.ap()[:, :, t0:t0 + C]),
                waits=list(st["xch"] or []), inc="sx")
            t_c0 = p.op("scalar", lambda e: e.copy(xch[:, :, 0:1], crow(0)),
                        waits=[t_x, t_init], inc="scalar")
            t_xc = p.op("scalar", lambda e: e.copy(crow(0), xch[:, :, C:C + 1]),
                        waits=[t_x, t_c0], inc="scalar")
            t_xd = p.op("vector", lambda e: e.tensor_sub(
                sA, xch[:, :, 1:C + 1], xch[:, :, 0:C]),
                waits=[t_c0, ("gwc", 0)] + gate, inc="vector")
            yield
            t_mk = t_mv = t_mr = None
            for j in range(JD):
                t_mk = p.op("vector", lambda e, j=j: e.scalar_tensor_tensor(
                    j2(kbf, j), j2(sA, j), vcol(V_TMK, j), j2(xch, j, 0, C),
                    ALU.mult, ALU.add), waits=[t_xd], inc="vector")
            for j in range(JD):
                t_mv = p.op("vector", lambda e, j=j: e.scalar_tensor_tensor(
                    j2(vbf, j), j2(sA, j), vcol(V_TMV, j), j2(xch, j, 0, C),
                    ALU.mult, ALU.add), waits=[t_xd], inc="vector")
            for j in range(JD):
                t_mr = p.op("vector", lambda e, j=j: e.scalar_tensor_tensor(
                    j2(sB, j), j2(sA, j), vcol(V_TMR, j), j2(xch, j, 0, C),
                    ALU.mult, ALU.add), waits=[t_xd], inc="vector")
            t_rsig = p.op("scalar", lambda e: e.activation(rbf, sB, AF.Sigmoid),
                          waits=[t_mr], inc="scalar")
            st["xch"] = [t_mr, t_xc]
            tapout("key_bf", kbf, t_mk); tapout("val_bf", vbf, t_mv)
            tapout("rcp_bf", rbf, t_rsig)

            yield
            grp_k = [None] * E
            for jE in range(E):
                bk = "E" if jE < 4 else "F"
                qoff = (jE % 4) * C
                for jd in range(JD):
                    tok = p.op("tensor", lambda e, jE=jE, jd=jd, bk=bk, qoff=qoff: e.matmul(
                        bank[bk][:, qoff:qoff + C],
                        wk_bf[:, jd, jE * 128:(jE + 1) * 128],
                        j2(kbf, jd), start=(jd == 0), stop=(jd == JD - 1)),
                        waits=([t_mk, t_wp[0], blast.get(bk)] if jd == 0 and jE % 4 == 0
                               else ([t_mk] if jd == 0 else ())),
                        inc=("tensor" if jd == JD - 1 else None))
                    if jd == JD - 1:
                        grp_k[jE] = tok
            yield
            t_s = t_m = None
            for jE in range(E):
                bk = "E" if jE < 4 else "F"
                qoff = (jE % 4) * C
                gw = grp_k[3] if jE < 4 else grp_k[7]
                t_s = p.op("vector", lambda e, jE=jE, bk=bk, qoff=qoff: e.tensor_scalar(
                    j2(sA, jE), bank[bk][:, qoff:qoff + C],
                    dcol(D_UPW, jE), dcol(D_2W, jE), ALU.add, ALU.max),
                    waits=[gw, t_dvec, t_mr], inc="vector")
                t_m = p.op("vector", lambda e, jE=jE, bk=bk, qoff=qoff: e.tensor_scalar(
                    j2(sB, jE), bank[bk][:, qoff:qoff + C],
                    dcol(D_UMW, jE), 0.0, ALU.add, ALU.min),
                    waits=[gw, t_rsig], inc="vector")
            blast["E"] = blast["F"] = t_m
            yield
            grp_v = [None] * E
            for jE in range(E):
                bk = "E" if jE < 4 else "F"
                qoff = (jE % 4) * C
                for jd in range(JD):
                    tok = p.op("tensor", lambda e, jE=jE, jd=jd, bk=bk, qoff=qoff: e.matmul(
                        bank[bk][:, qoff:qoff + C],
                        wv_bf[:, jd, jE * 128:(jE + 1) * 128],
                        j2(vbf, jd), start=(jd == 0), stop=(jd == JD - 1)),
                        waits=([t_mv, t_wp[1], blast.get(bk)] if jd == 0 and jE % 4 == 0
                               else ([t_mv] if jd == 0 else ())),
                        inc=("tensor" if jd == JD - 1 else None))
                    if jd == JD - 1:
                        grp_v[jE] = tok
            yield
            t_vev = None
            for half in range(2):
                bk = "E" if half == 0 else "F"
                gw = grp_v[3] if half == 0 else grp_v[7]
                t_vev = p.op("scalar", lambda e, half=half, bk=bk: e.copy(
                    sV[:, half * 4:half * 4 + 4, :].rearrange("p a c -> p (a c)"),
                    bank[bk][:]), waits=[gw], inc="scalar")
                blast[bk] = t_vev
            yield
            t_rg = None
            for jE in range(E):
                for jd in range(JD):
                    tok = p.op("tensor", lambda e, jE=jE, jd=jd: e.matmul(
                        j2(rps.ap(), jE), wr_bf[:, jd, jE * 128:(jE + 1) * 128],
                        j2(rbf, jd), start=(jd == 0), stop=(jd == JD - 1)),
                        waits=([t_rsig, t_wp[2], st["rps_free"]] if jd == 0 and jE == 0
                               else ([t_rsig] if jd == 0 else ())),
                        inc=("tensor" if jd == JD - 1 else None))
                    if jd == JD - 1:
                        t_rg = tok
            tapout("s", sA, t_s); tapout("m", sB, t_m)

            yield
            t_Ls = None
            for j in range(JD):
                init = 0.0 if first else ccol(1, j)
                t_Ls = p.op("vector", lambda e, j=j, init=init: e.tensor_tensor_scan(
                    j2(sC, j), j2(sA, j), j2(sA, j), init, ALU.add, ALU.bypass),
                    waits=[t_s], inc="vector")
            yield
            t_Lc = p.op("scalar", lambda e: e.copy(crow(1), sC[:, :, C - 1:C]),
                        waits=[t_Ls], inc="scalar")
            t_W = p.op("vector", lambda e: e.tensor_add(sB, sB, sC),
                       waits=[t_Ls, t_m], inc="vector")
            t_z0c = p.op("vector", (lambda e: e.memset(sZ[:, :, 0:1], NEG_BIG)) if first
                         else (lambda e: e.tensor_copy(sZ[:, :, 0:1], crow(2))),
                         waits=[t_W], inc="vector")
            yield
            t_Zs = None
            for j in range(JD):
                init = NEG_BIG if first else ccol(2, j)
                t_Zs = p.op("vector", lambda e, j=j, init=init: e.tensor_tensor_scan(
                    j2(sZ, j, 1, C + 1), j2(sB, j), j2(sB, j), init, ALU.max, ALU.bypass),
                    waits=[t_W, t_z0c], inc="vector")
            yield
            t_Zc = p.op("scalar", lambda e: e.copy(crow(2), sZ[:, :, C:C + 1]),
                        waits=[t_Zs], inc="scalar")
            t_d1 = p.op("vector", lambda e: e.tensor_sub(sA, sZ[:, :, 0:C], sZ[:, :, 1:C + 1]),
                        waits=[t_Zs, t_Ls], inc="vector")
            t_d2 = p.op("scalar", lambda e: e.activation(sA, sA, AF.Exp),
                        waits=[t_d1], inc="scalar")
            t_e1 = p.op("vector", lambda e: e.tensor_sub(sB, sB, sZ[:, :, 1:C + 1]),
                        waits=[t_Zs, t_d1], inc="vector")
            t_e2 = p.op("scalar", lambda e: e.activation(sB, sB, AF.Exp),
                        waits=[t_e1], inc="scalar")
            yield
            t_y = p.op("vector", lambda e: e.tensor_mul(sV, sB, sV),
                       waits=[t_e2, t_vev], inc="vector")
            tapout("Lpos", sC, t_Lc); tapout("Z", sZ[:, :, 1:C + 1], t_Zs)
            tapout("delta", sA, t_d2); tapout("ew", sB, t_e2)
            yield
            t_ns = None
            for j in range(JD):
                init = 0.0 if first else ccol(3, j)
                t_ns = p.op("vector", lambda e, j=j, init=init: e.tensor_tensor_scan(
                    j2(sC, j), j2(sA, j), j2(sV, j), init, ALU.mult, ALU.add),
                    waits=[t_y, t_d2, t_Lc], inc="vector")
            yield
            t_ds = None
            for j in range(JD):
                init = 0.0 if first else ccol(4, j)
                t_ds = p.op("vector", lambda e, j=j, init=init: e.tensor_tensor_scan(
                    j2(sZ, j, 0, C), j2(sA, j), j2(sB, j), init, ALU.mult, ALU.add),
                    waits=[t_y, t_d2, t_Zc], inc="vector")
            yield
            t_nc = p.op("scalar", lambda e: e.copy(crow(3), sC[:, :, C - 1:C]),
                        waits=[t_ns], inc="scalar")
            t_dc = p.op("scalar", lambda e: e.copy(crow(4), sZ[:, :, C - 1:C]),
                        waits=[t_ds], inc="scalar")
            tapout("n", sC, t_nc); tapout("d", sZ[:, :, 0:C], t_dc)
            t_ln = p.op("scalar", lambda e: e.activation(sV, sZ[:, :, 0:C], AF.Ln),
                        waits=[t_ds, t_ns, t_y], inc="scalar")
            t_rc = p.op("scalar", lambda e: e.activation(sV, sV, AF.Exp, scale=-1.0),
                        waits=[t_ln], inc="scalar")
            yield
            t_wkv = p.op("vector", lambda e: e.tensor_mul(sB, sC, sV),
                         waits=[t_rc, t_ns, t_ds], inc="vector")
            tapout("wkv", sB, t_wkv)
            t_wkvr = p.op("vector", lambda e, c=c: e.tensor_tensor(
                wkvr4[c], sB, rps.ap(), ALU.mult),
                waits=[t_wkv, t_rg], inc="vector")
            st["rps_free"] = t_wkvr
            seg_state[(q, c)] = dict(t_wkvr=t_wkvr, t_mk=t_mk, t_xd=t_xd)
            return t_wkvr

        def tail(q):
            t_wo = p.dma("gpsimd", lambda e: e.dma_start(wo_bf, woC.ap()),
                         waits=list(st["moe_done"]) + [("gwo", 0)], inc="gwp")
            t_x1b_last = None
            t_x1c_last = None
            t_g_last = None
            for c in range(NCH):
                t0 = q * TQ + c * C
                t_wkvr = seg_state[(q, c)]["t_wkvr"]
                t_x2 = p.dma("sync", lambda e, t0=t0: e.dma_start(
                    xch2, xH.ap()[:, :, t0:t0 + C]),
                    waits=list(st["moe_done"]) + [st.get("xch2_free")], inc="sx2")
                grp_o = [None] * JD
                for jd in range(JD):
                    bk = "A" if jd < 4 else "B"
                    qoff = (jd % 4) * C
                    for jE in range(E):
                        tok = p.op("tensor", lambda e, jd=jd, jE=jE, bk=bk, qoff=qoff, c=c: e.matmul(
                            bank[bk][:, qoff:qoff + C],
                            wo_bf[:, jE, jd * 128:(jd + 1) * 128],
                            j2(wkvr4[c], jE), start=(jE == 0), stop=(jE == E - 1)),
                            waits=([t_wkvr, t_wo, blast.get(bk)] if jE == 0 and jd % 4 == 0
                                   else ([t_wkvr] if jE == 0 else ())),
                            inc=("tensor" if jE == E - 1 else None))
                        if jE == E - 1:
                            grp_o[jd] = tok
                t_z0v = None
                for jd in range(JD):
                    bk = "A" if jd < 4 else "B"
                    qoff = (jd % 4) * C
                    gw = grp_o[3] if jd < 4 else grp_o[7]
                    t_z0v = p.op("vector", lambda e, jd=jd, bk=bk, qoff=qoff: e.scalar_tensor_tensor(
                        j2(sA, jd), bank[bk][:, qoff:qoff + C],
                        vcol(V_BO, jd), j2(xch2, jd), ALU.add, ALU.add),
                        waits=[gw, t_x2] +
                              ([seg_state[(q, NCH - 1)]["t_wkvr"]] if c == 0 else []),
                        inc="vector")
                blast["A"] = blast["B"] = t_z0v
                st["xch2_free"] = t_z0v
                tapout("z0", sA, t_z0v)
                t_zbf = p.op("scalar", lambda e: e.copy(kbf, sA), waits=[t_z0v], inc="scalar")
                t_zsq = p.op("scalar", lambda e: e.activation(vbf, sA, AF.Square),
                             waits=[t_z0v], inc="scalar")
                g1 = None
                for j in range(JD):
                    g1 = p.op("tensor", lambda e, j=j: e.matmul(
                        psC[0:1, 0:C], onescolb[:], j2(kbf, j),
                        start=(j == 0), stop=(j == JD - 1)),
                        waits=([t_zbf, t_onesb, blast.get("C")] if j == 0 else ()),
                        inc=("tensor" if j == JD - 1 else None))
                g2 = None
                for j in range(JD):
                    g2 = p.op("tensor", lambda e, j=j: e.matmul(
                        psC[0:1, C:2 * C], onescolb[:], j2(vbf, j),
                        start=(j == 0), stop=(j == JD - 1)),
                        waits=([t_zsq] if j == 0 else ()),
                        inc=("tensor" if j == JD - 1 else None))
                t_g_last = g2
                rv = rows.ap()
                mu_r, ex2_r, rstd_r, mur_r = (rv[:, 0:C], rv[:, C:2 * C],
                                              rv[:, 2 * C:3 * C], rv[:, 3 * C:4 * C])
                t_mu = p.op("vector", lambda e: e.tensor_scalar_mul(mu_r, psC[0:1, 0:C], 1.0 / D),
                            waits=[g1, g2], inc="vector")
                t_ex = p.op("vector", lambda e: e.tensor_scalar_mul(ex2_r, psC[0:1, C:2 * C], 1.0 / D),
                            waits=[g2], inc="vector")
                blast["C"] = t_ex
                t_m2_ = p.op("vector", lambda e: e.tensor_mul(rstd_r, mu_r, mu_r),
                             waits=[t_mu], inc="vector")
                t_var = p.op("vector", lambda e: e.tensor_sub(rstd_r, ex2_r, rstd_r),
                             waits=[t_m2_, t_ex], inc="vector")
                t_lv = p.op("scalar", lambda e: e.activation(rstd_r, rstd_r, AF.Ln, bias=epsc[0:1, 0:1]),
                            waits=[t_var, t_eps], inc="scalar")
                t_rs = p.op("scalar", lambda e: e.activation(rstd_r, rstd_r, AF.Exp, scale=-0.5),
                            waits=[t_lv], inc="scalar")
                t_mr_ = p.op("vector", lambda e: e.tensor_mul(mur_r, mu_r, rstd_r),
                             waits=[t_rs], inc="vector")
                t_b1m = p.op("tensor", lambda e: e.matmul(
                    psD[:, 0:C], onesrow[:], rstd_r, start=True, stop=True),
                    waits=[t_rs, tL["or"], blast.get("D")], inc="tensor")
                t_b2m = p.op("tensor", lambda e: e.matmul(
                    psD[:, C:2 * C], onesrow[:], mur_r, start=True, stop=True),
                    waits=[t_mr_], inc="tensor")
                x1v = x1.ap()[:, :, c * C:(c + 1) * C]
                x1bv = x1bf.ap()[:, :, c * C:(c + 1) * C]
                t_x1a = p.op("vector", lambda e, x1v=x1v: e.tensor_tensor(
                    x1v, sA, psD.ap()[:, 0:C].unsqueeze(1).broadcast_to([128, JD, C]),
                    ALU.mult),
                    waits=[t_b1m, t_b2m] + (list(st["x1_free"]) if c == 0 else []),
                    inc="vector")
                t_x1b = p.op("vector", lambda e, x1v=x1v: e.tensor_tensor(
                    x1v, x1v, psD.ap()[:, C:2 * C].unsqueeze(1).broadcast_to([128, JD, C]),
                    ALU.subtract), waits=[t_x1a], inc="vector")
                blast["D"] = t_x1b
                t_x1c = p.op("scalar", lambda e, x1bv=x1bv, x1v=x1v: e.copy(x1bv, x1v),
                             waits=[t_x1b] + (list(st["x1bf_free"]) if c == 0 else []),
                             inc="scalar")
                t_x1b_last, t_x1c_last = t_x1b, t_x1c
            w2_war[0].append(st["xch2_free"])
            st["tail_last_x1b"] = t_x1b_last
            st["x1c_last"] = t_x1c_last
            st["slots_free"] = [t_x1b_last, t_x1c_last, t_g_last]
            tapout("x1", x1.ap(), t_x1b_last)

        def ensure_w1(q, e_, s_, t_w1dma, buf_of):
            key = (e_, s_)
            if key not in t_w1dma:
                buf = nw1[0] % 2; nw1[0] += 1
                if q == 0:
                    tok = p.dma("gpsimd", lambda en, e_=e_, s_=s_, buf=buf: en.dma_start(
                        w1s[buf], w1H.ap()[e_, s_]),
                        waits=list(w1_war[buf]), inc=("gw1a" if buf == 0 else "gw1b"))
                    cw = p.dma("sync", lambda en, e_=e_, s_=s_, buf=buf: en.dma_start(
                        w1C.ap()[e_, s_], w1s[buf]), waits=[tok], inc="gwc")
                    w1_war[buf] = [cw]
                else:
                    tok = p.dma("sync", lambda en, e_=e_, s_=s_, buf=buf: en.dma_start(
                        w1s[buf], w1C.ap()[e_, s_]),
                        waits=list(w1_war[buf]) + [cache_done[0]],
                        inc=("cr1a" if buf == 0 else "cr1b"))
                    w1_war[buf] = []
                t_w1dma[key] = tok
                buf_of[key] = buf
            return t_w1dma[key], buf_of[key]

        def ensure_w2(q, e_, hs, t_w2dma, buf_of):
            key = (e_, hs)
            if key not in t_w2dma:
                buf = nw2[0] % 2; nw2[0] += 1
                if q == 0:
                    tok = p.dma("gpsimd", lambda en, e_=e_, hs=hs, buf=buf: en.dma_start(
                        w2s[buf], w2H.ap()[e_, hs // 2, :, (hs % 2) * 16:(hs % 2) * 16 + 16, :]),
                        waits=list(w2_war[buf]), inc=("gw2a" if buf == 0 else "gw2b"))
                    cw = p.dma("sync", lambda en, e_=e_, hs=hs, buf=buf: en.dma_start(
                        w2C.ap()[e_, hs // 2, :, (hs % 2) * 16:(hs % 2) * 16 + 16, :],
                        w2s[buf]), waits=[tok], inc="gwc")
                    w2_war[buf] = [cw]
                else:
                    tok = p.dma("sync", lambda en, e_=e_, hs=hs, buf=buf: en.dma_start(
                        w2s[buf], w2C.ap()[e_, hs // 2, :, (hs % 2) * 16:(hs % 2) * 16 + 16, :]),
                        waits=list(w2_war[buf]) + [cache_done[0]],
                        inc=("cr2a" if buf == 0 else "cr2b"))
                    w2_war[buf] = []
                t_w2dma[key] = tok
                buf_of[key] = buf
            return t_w2dma[key], buf_of[key]

        for q in range(NQ):
            if q == 0:
                for c in range(NCH):
                    seg(0, c)
                st["moe_done"] = []
                tail(0)

            t_x1_ready = [st["tail_last_x1b"]]
            if q == 1:
                cache_done[0] = ("gwc", p.cnt["gwc"])
            glg = None
            for j in range(JD):
                glg = p.op("tensor", lambda e, j=j: e.matmul(
                    psD[0:E, 0:TQ], wqs[:, j, :], x1.ap()[:, j, :],
                    start=(j == 0), stop=(j == JD - 1)),
                    waits=(t_x1_ready + [tL["wq"], blast.get("D")] if j == 0 else ()),
                    inc=("tensor" if j == JD - 1 else None))
            t_lg = p.op("vector", lambda e: e.tensor_scalar(
                logits_sb[:], psD[0:E, 0:TQ], brs[:, 0:1], None, ALU.add),
                waits=[glg, tL["br"]], inc="vector")
            blast["D"] = t_lg
            tapout("logits", logits_sb.ap(), t_lg)
            t_tr = None
            for tt in range(NT):
                t_tr = p.op("tensor", lambda e, tt=tt: e.transpose(
                    psC[:, tt * E:(tt + 1) * E], logits_sb[:, tt * 128:(tt + 1) * 128],
                    id128[0:E, 0:E]),
                    waits=([t_lg, tL["id"], blast.get("C")] if tt == 0 else [t_lg]),
                    inc="tensor")
            t_lgT = p.op("vector", lambda e: e.tensor_copy(
                lgT[:], psC[:, 0:NT * E].rearrange("p (t e) -> p t e", t=NT)),
                waits=[t_tr], inc="vector")
            blast["C"] = t_lgT
            t_rwT = []
            for tt in range(NT):
                lgv = lgT.ap()[:, tt, :]
                ind1 = bctr.ap()[:, 0, 0:E]
                mskd = bctr.ap()[:, 1, 0:E]
                t_m1 = p.op("vector", lambda e, lgv=lgv: e.reduce_max(
                    srow[:, 0:1], lgv, axis=AX.X), waits=[t_lgT], inc="vector")
                t_i1 = p.op("vector", lambda e, lgv=lgv, ind1=ind1: e.tensor_scalar(
                    ind1, lgv, srow[:, 0:1], None, ALU.is_ge), waits=[t_m1], inc="vector")
                t_mk_ = p.op("vector", lambda e, lgv=lgv, ind1=ind1, mskd=mskd: e.scalar_tensor_tensor(
                    mskd, ind1, NEG_BIG, lgv, ALU.mult, ALU.add), waits=[t_i1], inc="vector")
                t_m2 = p.op("vector", lambda e, mskd=mskd: e.reduce_max(
                    srow[:, 1:2], mskd, axis=AX.X), waits=[t_mk_], inc="vector")
                t_dm = p.op("vector", lambda e: e.tensor_sub(
                    srow[:, 2:3], srow[:, 0:1], srow[:, 1:2]), waits=[t_m2], inc="vector")
                t_w1_ = p.op("scalar", lambda e: e.activation(
                    srow[:, 2:3], srow[:, 2:3], AF.Sigmoid), waits=[t_dm], inc="scalar")
                t_w2_ = p.op("vector", lambda e: e.tensor_scalar(
                    srow[:, 3:4], srow[:, 2:3], 1.0, -1.0, ALU.subtract, ALU.mult),
                    waits=[t_w1_], inc="vector")
                t_i2 = p.op("vector", lambda e, mskd=mskd: e.tensor_scalar(
                    mskd, mskd, srow[:, 1:2], None, ALU.is_ge), waits=[t_m2, t_w2_], inc="vector")
                t_r1 = p.op("vector", lambda e, ind1=ind1: e.tensor_scalar(
                    ind1, ind1, srow[:, 2:3], None, ALU.mult), waits=[t_w1_, t_mk_], inc="vector")
                t_r2 = p.op("vector", lambda e, lgv=lgv, ind1=ind1, mskd=mskd: e.scalar_tensor_tensor(
                    lgv, mskd, srow[:, 3:4], ind1, ALU.mult, ALU.add),
                    waits=[t_r1, t_i2], inc="vector")
                t_rwT.append(t_r2)
            t_tb = None
            for tt in range(NT):
                t_tb = p.op("tensor", lambda e, tt=tt: e.transpose(
                    psC[0:E, tt * 128:(tt + 1) * 128], lgT.ap()[:, tt, :], id128[:]),
                    waits=([t_rwT[tt], blast.get("C")] if tt == 0 else [t_rwT[tt]]),
                    inc="tensor")
            t_rwf = p.op("vector", lambda e: e.tensor_copy(rwfm[:], psC[0:E, 0:TQ]),
                         waits=[t_tb], inc="vector")
            blast["C"] = t_rwf
            tapout("rw", rwfm.ap(), t_rwf)

            t_w1dma = {}; t_w2dma = {}
            w1_buf_of = {}; w2_buf_of = {}
            t_rwbc_prev = None
            t_hev = None
            t_tg = None
            moe_last_h_mm = None
            for e_ in range(E):
                for s_ in range(8):
                    tok_dma, buf = ensure_w1(q, e_, s_, t_w1dma, w1_buf_of)
                    for fc in range(4):
                        jf = s_ * 4 + fc
                        bk = "ABCD"[jf % 4]
                        grp = None
                        for jd in range(JD):
                            grp = p.op("tensor", lambda e, jd=jd, fc=fc, buf=buf, bk=bk: e.matmul(
                                bank[bk][:, 0:TQ], w1s[buf][:, jd, fc * 128:(fc + 1) * 128],
                                x1bf.ap()[:, jd, :], start=(jd == 0), stop=(jd == JD - 1)),
                                waits=([tok_dma, st["x1c_last"], blast.get(bk)] if jd == 0 else ()),
                                inc=("tensor" if jd == JD - 1 else None))
                        t_hev = p.op("scalar", lambda e, jf=jf, e_=e_, bk=bk: e.activation(
                            hsb[:, jf, :], bank[bk][:, 0:TQ], AF.Relu,
                            bias=b1s.ap()[:, e_:e_ + 1, jf:jf + 1].rearrange("p a b -> p (a b)")),
                            waits=[grp, tL["b1"]], inc="scalar")
                        blast[bk] = t_hev
                        w1_war[buf].append(grp)
                        moe_last_h_mm = grp
                    nxt = (e_, s_ + 1) if s_ + 1 < 8 else ((e_ + 1, 0) if e_ + 1 < E else None)
                    if nxt is not None:
                        ensure_w1(q, nxt[0], nxt[1], t_w1dma, w1_buf_of)
                    pump()
                if e_ == 0:
                    tapout("h0", hsb, t_hev)
                t_h_done = t_hev
                t_rbm = p.op("tensor", lambda e, e_=e_: e.matmul(
                    psD[:, 0:TQ], sels[:, e_, :], rwfm[:], start=True, stop=True),
                    waits=[t_rwf, tL["sel"], blast.get("D")], inc="tensor")
                t_rwbc = p.op("scalar", lambda e: e.copy(rwbc[:], psD[:, 0:TQ]),
                              waits=[t_rbm, t_rwbc_prev], inc="scalar")
                blast["D"] = t_rwbc
                for dc in range(8):
                    bk = "CDAB"[dc % 4]
                    grp = None
                    for hs in range(2):
                        tok_dma, buf = ensure_w2(q, e_, dc * 2 + hs, t_w2dma, w2_buf_of)
                        for jfi in range(16):
                            jf = hs * 16 + jfi
                            grp = p.op("tensor", lambda e, jf=jf, jfi=jfi, buf=buf, bk=bk: e.matmul(
                                bank[bk][:, 0:TQ], w2s[buf][:, jfi, :], hsb[:, jf, :],
                                start=(jf == 0), stop=(jf == JF - 1)),
                                waits=([tok_dma, blast.get(bk)] if jf == 0
                                       else ([tok_dma] if jfi == 0 else
                                             ([t_h_done] if jf == 8 else ()))),
                                inc=("tensor" if jfi == 15 else None))
                        w2_war[buf].append(grp)
                    nxt_hs = dc * 2 + 2
                    if nxt_hs < 16:
                        ensure_w2(q, e_, nxt_hs, t_w2dma, w2_buf_of)
                    elif e_ + 1 < E:
                        ensure_w2(q, e_ + 1, 0, t_w2dma, w2_buf_of)
                    pump()
                    t_oc = p.op("scalar", lambda e, bk=bk, e_=e_, dc=dc: e.activation(
                        otmp[:], bank[bk][:, 0:TQ], AF.Identity,
                        bias=b2s.ap()[:, e_:e_ + 1, dc:dc + 1].rearrange("p a b -> p (a b)")),
                        waits=[grp, tL["b2"], t_tg], inc="scalar")
                    t_om = p.op("gpsimd", lambda e: e.tensor_mul(otmp[:], otmp[:], rwbc[:]),
                                waits=[t_oc, t_rwbc], inc="gpsimd")
                    tg = tgt2.ap()[:, dc, :]
                    if e_ == 0:
                        t_tg = p.op("gpsimd", lambda e, tg=tg, dc=dc: e.tensor_add(
                            tg, otmp[:], x1.ap()[:, dc, :]),
                            waits=[t_om, st["tgt2_free"]] + t_x1_ready, inc="gpsimd")
                    else:
                        t_tg = p.op("gpsimd", lambda e, tg=tg: e.tensor_add(tg, tg, otmp[:]),
                                    waits=[t_om], inc="gpsimd")
                    blast[bk] = t_tg
                t_rwbc_prev = t_tg
                if e_ % 2 == 0 and q + 1 < NQ:
                    pending.append(seg_gen(q + 1, e_ // 2))
                pump(6)
            while pending:
                pump(100)
            t_tgt2_done = t_tg
            st["x1bf_free"] = [t_hev]
            st["moe_done"] = [moe_last_h_mm, t_tgt2_done]
            tapout("tgt2", tgt2.ap(), t_tgt2_done)

            def ln_pass(t_src, apply_gb):
                src_ap = tgt2.ap()
                zbf_t = hsb[:, 0:JD, :]
                zsq_t = hsb[:, JD:2 * JD, :]
                t_zb = p.op("scalar", lambda e: e.copy(zbf_t, src_ap),
                            waits=list(t_src) + list(st["moe_done"]), inc="scalar")
                t_zq = p.op("scalar", lambda e: e.activation(zsq_t, src_ap, AF.Square),
                            waits=list(t_src), inc="scalar")
                g1 = None
                for j in range(JD):
                    g1 = p.op("tensor", lambda e, j=j: e.matmul(
                        psA[0:1, 0:TQ], onescolb[:], zbf_t[:, j, :],
                        start=(j == 0), stop=(j == JD - 1)),
                        waits=([t_zb, blast.get("A")] if j == 0 else ()),
                        inc=("tensor" if j == JD - 1 else None))
                g2 = None
                for j in range(JD):
                    g2 = p.op("tensor", lambda e, j=j: e.matmul(
                        psB[0:1, 0:TQ], onescolb[:], zsq_t[:, j, :],
                        start=(j == 0), stop=(j == JD - 1)),
                        waits=([t_zq, blast.get("B")] if j == 0 else ()),
                        inc=("tensor" if j == JD - 1 else None))
                rv2 = rows.ap()
                mu_r = rv2[:, 0:TQ]
                ex_r = rv2[:, TQ:2 * TQ]
                sq_r = logits_sb.ap()[0:1, 0:TQ]
                t_mu = p.op("vector", lambda e: e.tensor_scalar_mul(mu_r, psA[0:1, 0:TQ], 1.0 / D),
                            waits=[g1], inc="vector")
                blast["A"] = t_mu
                t_ex = p.op("vector", lambda e: e.tensor_scalar_mul(ex_r, psB[0:1, 0:TQ], 1.0 / D),
                            waits=[g2], inc="vector")
                blast["B"] = t_ex
                t_q1 = p.op("vector", lambda e: e.tensor_mul(sq_r, mu_r, mu_r),
                            waits=[t_mu, st.get("ln_rw_done")], inc="vector")
                t_q2 = p.op("vector", lambda e: e.tensor_sub(ex_r, ex_r, sq_r),
                            waits=[t_q1, t_ex], inc="vector")
                t_lv = p.op("scalar", lambda e: e.activation(ex_r, ex_r, AF.Ln, bias=epsc[0:1, 0:1]),
                            waits=[t_q2, t_eps], inc="scalar")
                t_rs = p.op("scalar", lambda e: e.activation(ex_r, ex_r, AF.Exp, scale=-0.5),
                            waits=[t_lv], inc="scalar")
                t_mr2 = p.op("vector", lambda e: e.tensor_mul(mu_r, mu_r, ex_r),
                             waits=[t_rs], inc="vector")
                t_bm1 = p.op("tensor", lambda e: e.matmul(
                    psC[:, 0:TQ], onesrow[:], ex_r, start=True, stop=True),
                    waits=[t_rs, blast.get("C")], inc="tensor")
                t_bm2 = p.op("tensor", lambda e: e.matmul(
                    psD[:, 0:TQ], onesrow[:], mu_r, start=True, stop=True),
                    waits=[t_mr2, blast.get("D")], inc="tensor")
                t_a1 = p.op("vector", lambda e: e.tensor_tensor(
                    src_ap, src_ap,
                    psC.ap()[:, 0:TQ].unsqueeze(1).broadcast_to([128, JD, TQ]), ALU.mult),
                    waits=[t_bm1, t_zb, t_zq], inc="vector")
                t_a2 = p.op("vector", lambda e: e.tensor_tensor(
                    src_ap, src_ap,
                    psD.ap()[:, 0:TQ].unsqueeze(1).broadcast_to([128, JD, TQ]), ALU.subtract),
                    waits=[t_bm2, t_a1], inc="vector")
                blast["C"] = t_a2
                blast["D"] = t_a2
                t_last = t_a2
                if apply_gb:
                    for j in range(JD):
                        t_last = p.op("vector", lambda e, j=j: e.tensor_scalar(
                            j2(src_ap, j), j2(src_ap, j), vcol(V_GAM, j), vcol(V_BET, j),
                            ALU.mult, ALU.add), waits=[t_a2], inc="vector")
                return t_last

            st["ln_rw_done"] = None
            t_ln1 = ln_pass([t_tgt2_done], apply_gb=True)
            t_z3 = p.op("vector", lambda e: e.tensor_add(tgt2.ap(), x1.ap(), tgt2.ap()),
                        waits=[t_ln1], inc="vector")
            st["x1_free"] = [t_z3]
            t_ln2 = ln_pass([t_z3], apply_gb=False)
            t_out = p.dma("sync", lambda e, q=q: e.dma_start(
                outH.ap()[:, :, q * TQ:(q + 1) * TQ], tgt2.ap()),
                waits=[t_ln2], inc="out")
            st["tgt2_free"] = t_out
            st["moe_done"] = [t_ln2]

            if q + 1 < NQ:
                tail(q + 1)

        p.op("sync", None, waits=[(s_, p.cnt[s_]) for s_ in P.DMA_SEMS if p.cnt[s_]])

        p.emit(blk)
    return nc, tap_t



def _prep_shared(inp):
    f32 = np.float32
    def A(k):
        return np.asarray(inp[k], f32)
    def projH(Wm):
        return np.ascontiguousarray(Wm.T.reshape(JD, 128, D).transpose(1, 0, 2))
    d = {}
    d["wkH"] = projH(A("L0_attn_Wk")); d["wvH"] = projH(A("L0_attn_Wv"))
    d["wrH"] = projH(A("L0_attn_Wr")); d["woH"] = projH(A("L0_attn_Wo"))
    d["w1H"] = np.ascontiguousarray(
        A("L0_ffn_W1").reshape(E, 8, 512, JD, 128).transpose(0, 1, 4, 3, 2))
    d["w2H"] = np.ascontiguousarray(
        A("L0_ffn_W2").reshape(E, 8, 128, JF, 128).transpose(0, 1, 4, 3, 2))
    vecs = np.stack([A("L0_attn_u"), A("L0_attn_w"), A("L0_attn_tmk"),
                     A("L0_attn_tmv"), A("L0_attn_tmr"), A("L0_attn_bo"),
                     A("L0_ffn_gamma"), A("L0_ffn_beta")], 0)
    d["vecsH"] = np.ascontiguousarray(vecs.reshape(8, JD, 128).transpose(2, 0, 1))
    d["b1H"] = np.ascontiguousarray(A("L0_ffn_b1").reshape(E, JF, 128).transpose(2, 0, 1))
    d["b2H"] = np.ascontiguousarray(A("L0_ffn_b2").reshape(E, E, 128).transpose(2, 0, 1))
    d["wqH"] = np.ascontiguousarray(A("L0_ffn_Wr").T.reshape(JD, 128, E).transpose(1, 0, 2))
    d["brH"] = np.ascontiguousarray(A("L0_ffn_br").reshape(E, 1))
    d["id128H"] = np.eye(128, dtype=f32)
    d["onesH"] = np.ones((128, 1), f32)
    d["onesrowH"] = np.ones((1, 128), f32)
    sel = np.zeros((E, E, 128), f32)
    for e_ in range(E):
        sel[e_, e_, :] = 1.0
    d["selH"] = sel
    return d


def kernel(**inputs):
    x = np.asarray(inputs["L0_attn_x"], np.float32)
    B, T, _ = x.shape
    shared = _prep_shared(inputs)
    nc, _ = build(T=T)
    in_maps = []
    for b in range(B):
        m = dict(shared)
        m["xH"] = np.ascontiguousarray(x[b].T.reshape(JD, 128, T).transpose(1, 0, 2))
        in_maps.append(m)
    res = run_bass_kernel_spmd(nc, in_maps, core_ids=list(range(8)))
    out = np.empty((B, T, D), np.float32)
    for b in range(B):
        oH = np.asarray(res.results[b]["outH"])
        out[b] = oH.transpose(1, 0, 2).reshape(D, T).T
    return out


# version 13
# speedup vs baseline: 1.1981x; 1.1981x over previous
import numpy as np
import concourse.bass as bass
import concourse.mybir as mybir
from concourse.bass_utils import run_bass_kernel_spmd

F32 = mybir.dt.float32
BF16 = mybir.dt.bfloat16
AF = mybir.ActivationFunctionType
ALU = mybir.AluOpType
AX = mybir.AxisListType

D = 1024
JD = 8
E = 8
F = 4096
JF = 32
LN_EPS = 1e-5
NEG_BIG = -1.0e30


class Op:
    __slots__ = ("eng", "fn", "waits", "tok")
    def __init__(self, eng, fn, waits, tok):
        self.eng = eng; self.fn = fn; self.waits = waits; self.tok = tok

class P:
    ENGS = ("sync", "gpsimd", "vector", "scalar", "tensor")
    DMA_SEMS = frozenset({"init", "sx", "gwp", "gw1a", "gw1b", "gw2a", "gw2b",
                          "cr1a", "cr1b", "cr2a", "cr2b", "gwc", "out"})

    def __init__(self, sems):
        self.sems = sems
        self.cnt = {k: 0 for k in sems}
        self.ops = []
        self.tap_gate = None

    def op(self, eng, fn=None, waits=(), inc=None, amt=None):
        w = [x for x in waits if x is not None]
        if self.tap_gate is not None:
            w.append(self.tap_gate)
        w = tuple((s, self.cnt[s]) if s in self.DMA_SEMS else (s, v) for (s, v) in w)
        tok = None
        if inc is not None:
            a = amt if amt is not None else 1
            self.cnt[inc] += a
            tok = (inc, self.cnt[inc], a)
        self.ops.append(Op(eng, fn, w, tok))
        return None if tok is None else (tok[0], tok[1])

    def dma(self, eng, fn, waits=(), inc=None):
        return self.op(eng, fn, waits=waits, inc=(inc or eng), amt=16)

    def emit(self, block):
        for eng in self.ENGS:
            getattr(block, eng)(self._mk_emitter(eng))

    def _mk_emitter(self, eng):
        def run(e):
            waited = {}
            for o in self.ops:
                if o.eng != eng:
                    continue
                for (s, v) in o.waits:
                    if waited.get(s, -1) >= v:
                        continue
                    e.wait_ge(self.sems[s], v)
                    waited[s] = v
                if o.fn is None:
                    continue
                inst = o.fn(e)
                if o.tok is not None:
                    inst.then_inc(self.sems[o.tok[0]], o.tok[2])
        return run



def build(T=4096, TQ=512, C=256, taps=()):
    taps = set(taps)
    NQ = T // TQ
    NCH = TQ // C
    NT = TQ // 128
    nc = bass.Bass("TRN2", target_bir_lowering=False)

    xH = nc.dram_tensor("xH", [128, JD, T], F32, kind="ExternalInput")
    wkH = nc.dram_tensor("wkH", [128, JD, D], F32, kind="ExternalInput")
    wvH = nc.dram_tensor("wvH", [128, JD, D], F32, kind="ExternalInput")
    wrH = nc.dram_tensor("wrH", [128, JD, D], F32, kind="ExternalInput")
    woH = nc.dram_tensor("woH", [128, JD, D], F32, kind="ExternalInput")
    w1H = nc.dram_tensor("w1H", [E, 8, 128, JD, 512], F32, kind="ExternalInput")
    w2H = nc.dram_tensor("w2H", [E, 8, 128, JF, 128], F32, kind="ExternalInput")
    vecsH = nc.dram_tensor("vecsH", [128, 8, JD], F32, kind="ExternalInput")
    b1H = nc.dram_tensor("b1H", [128, E, JF], F32, kind="ExternalInput")
    b2H = nc.dram_tensor("b2H", [128, E, E], F32, kind="ExternalInput")
    wqH = nc.dram_tensor("wqH", [128, JD, E], F32, kind="ExternalInput")
    brH = nc.dram_tensor("brH", [E, 1], F32, kind="ExternalInput")
    id128H = nc.dram_tensor("id128H", [128, 128], F32, kind="ExternalInput")
    onesH = nc.dram_tensor("onesH", [128, 1], F32, kind="ExternalInput")
    onesrowH = nc.dram_tensor("onesrowH", [1, 128], F32, kind="ExternalInput")
    selH = nc.dram_tensor("selH", [E, E, 128], F32, kind="ExternalInput")
    outH = nc.dram_tensor("outH", [128, JD, T], F32, kind="ExternalOutput")
    w1C = nc.dram_tensor("w1C", [E, 8, 128, JD, 512], BF16)
    w2C = nc.dram_tensor("w2C", [E, 8, 128, JF, 128], BF16)

    tap_t = {}
    tap_done = set()
    def mktap(name, shape, dtype=F32):
        if name in taps:
            tap_t[name] = nc.dram_tensor("tap_" + name, shape, dtype, kind="ExternalOutput")

    mktap("key_bf", [128, JD, C], BF16); mktap("val_bf", [128, JD, C], BF16)
    mktap("rcp_bf", [128, JD, C], BF16)
    mktap("s", [128, JD, C]); mktap("m", [128, JD, C])
    mktap("Lpos", [128, JD, C]); mktap("W", [128, JD, C]); mktap("Z", [128, JD, C])
    mktap("delta", [128, JD, C]); mktap("ew", [128, JD, C])
    mktap("n", [128, JD, C]); mktap("d", [128, JD, C]); mktap("wkv", [128, JD, C])
    mktap("z0", [128, JD, C]); mktap("x1", [128, JD, TQ])
    mktap("logits", [E, TQ]); mktap("rw", [E, TQ])
    mktap("h0", [128, JF, TQ], BF16); mktap("tgt2", [128, JD, TQ])

    SCN = JD * C
    ARENA = 8 * (SCN + JD * (C + 1)) // 8
    oA, oB, oC = 0, SCN, 2 * SCN
    oZ = 3 * SCN
    oW = oZ + JD * (C + 1)
    oV = oW + SCN
    oKVR = oV + SCN
    oWKVR = oKVR + 3 * SCN // 2
    attn_end = oWKVR + SCN // 2
    oH = 0
    oW1 = oH + JF * TQ // 2
    oW2 = oW1 + 2 * (JD * 256)
    moe_end = oW2 + 2 * (JF * 64)
    ARENA = max(attn_end, moe_end)

    from contextlib import ExitStack
    with ExitStack() as stk:
        en = stk.enter_context
        x1 = en(nc.sbuf_tensor("x1", [128, JD, TQ], F32))
        x1bf = en(nc.sbuf_tensor("x1bf", [128, JD, TQ], BF16))
        tgt2 = en(nc.sbuf_tensor("tgt2", [128, JD, TQ], F32))
        xch = en(nc.sbuf_tensor("xch", [128, JD, C + 1], F32))
        wproj = en(nc.sbuf_tensor("wproj", [128, 4 * JD, D], BF16))
        vecs = en(nc.sbuf_tensor("vecs", [128, 8, JD], F32))
        dvec = en(nc.sbuf_tensor("dvec", [128, 3, JD], F32))
        b1s = en(nc.sbuf_tensor("b1s", [128, E, JF], F32))
        b2s = en(nc.sbuf_tensor("b2s", [128, E, E], F32))
        wqs = en(nc.sbuf_tensor("wqs", [128, JD, E], F32))
        brs = en(nc.sbuf_tensor("brs", [E, 1], F32))
        id128 = en(nc.sbuf_tensor("id128", [128, 128], F32))
        onescol = en(nc.sbuf_tensor("onescol", [128, 1], F32))
        onescolb = en(nc.sbuf_tensor("onescolb", [128, 1], BF16))
        onesrow = en(nc.sbuf_tensor("onesrow", [1, 128], F32))
        sels = en(nc.sbuf_tensor("sels", [E, E, 128], F32))
        carry = en(nc.sbuf_tensor("carry", [128, 5, JD], F32))
        bct = en(nc.sbuf_tensor("bct", [128, 2, C], F32))
        bcq = en(nc.sbuf_tensor("bcq", [128, 2, TQ], F32))
        rwbc = en(nc.sbuf_tensor("rwbc", [128, TQ], F32))
        otmp = en(nc.sbuf_tensor("otmp", [128, TQ], F32))
        logits_sb = en(nc.sbuf_tensor("logits", [E, TQ], F32))
        rwfm = en(nc.sbuf_tensor("rwfm", [E, TQ], F32))
        lgT = en(nc.sbuf_tensor("lgT", [128, NT, E], F32))
        rows = en(nc.sbuf_tensor("rows", [1, 4 * max(C, TQ // 2)], F32))
        srow = en(nc.sbuf_tensor("srow", [128, 8], F32))
        epsc = en(nc.sbuf_tensor("epsc", [128, 1], F32))
        arena = en(nc.sbuf_tensor("arena", [128, ARENA], F32))
        psA = en(nc.psum_tensor("psA", [128, 512], F32))
        psB = en(nc.psum_tensor("psB", [128, 512], F32))
        psC = en(nc.psum_tensor("psC", [128, 512], F32))
        psD = en(nc.psum_tensor("psD", [128, 512], F32))
        rps = en(nc.psum_tensor("rps", [128, JD, C], F32))
        S0 = en(nc.semaphore("sm_sync"))
        S1 = en(nc.semaphore("sm_gpsimd"))
        S2 = en(nc.semaphore("sm_vector"))
        S3 = en(nc.semaphore("sm_scalar"))
        S4 = en(nc.semaphore("sm_tensor"))
        S5 = en(nc.semaphore("sm_out"))
        S6 = en(nc.semaphore("sm_gwp"))
        S7 = en(nc.semaphore("sm_gw1a"))
        S8 = en(nc.semaphore("sm_gw1b"))
        S9 = en(nc.semaphore("sm_gw2a"))
        S10 = en(nc.semaphore("sm_gw2b"))
        S11 = en(nc.semaphore("sm_init"))
        S12 = en(nc.semaphore("sm_sx"))
        S13 = en(nc.semaphore("sm_gwc"))
        S14 = en(nc.semaphore("sm_cr1a"))
        S15 = en(nc.semaphore("sm_cr1b"))
        S16 = en(nc.semaphore("sm_cr2a"))
        S17 = en(nc.semaphore("sm_cr2b"))
        blk = en(nc.Block())
        sems = {"sync": S0, "gpsimd": S1, "vector": S2, "scalar": S3,
                "tensor": S4, "out": S5, "gwp": S6, "gw1a": S7, "gw1b": S8,
                "gw2a": S9, "gw2b": S10, "init": S11, "sx": S12, "gwc": S13,
                "cr1a": S14, "cr1b": S15, "cr2a": S16, "cr2b": S17}
        p = P(sems)

        def carve(off, *shape, dt=F32):
            n = int(np.prod(shape))
            if dt == BF16:
                ap = arena.ap()[:, off:off + (n + 1) // 2].bitcast(BF16)
            else:
                ap = arena.ap()[:, off:off + n]
            if len(shape) > 1:
                dims = " ".join(f"d{i}" for i in range(len(shape)))
                kw = {f"d{i}": s_ for i, s_ in enumerate(shape)}
                ap = ap.rearrange(f"p ({dims}) -> p {dims}", **kw)
            return ap

        sA = carve(oA, JD, C); sB = carve(oB, JD, C); sC = carve(oC, JD, C)
        sZ = carve(oZ, JD, C + 1)
        sW = carve(oW, JD, C); sV = carve(oV, JD, C)
        kbf = carve(oKVR, JD, C, dt=BF16)
        vbf = carve(oKVR + SCN // 2, JD, C, dt=BF16)
        rbf = carve(oKVR + SCN, JD, C, dt=BF16)
        wkvrbf = carve(oWKVR, JD, C, dt=BF16)
        hsb = carve(oH, JF, TQ, dt=BF16)
        w1s = [carve(oW1 + i * JD * 256, JD, 512, dt=BF16) for i in range(2)]
        w2s = [carve(oW2 + i * JF * 64, JF, 128, dt=BF16) for i in range(2)]

        wpv = wproj.ap().rearrange("p (m j) d -> p m j d", m=4)
        wk_bf, wv_bf, wr_bf, wo_bf = wpv[:, 0], wpv[:, 1], wpv[:, 2], wpv[:, 3]

        V_U, V_W, V_TMK, V_TMV, V_TMR, V_BO, V_GAM, V_BET = range(8)
        D_UPW, D_UMW, D_2W = range(3)

        def j2(ap3, j, lo=None, hi=None):
            v = ap3[:, j:j + 1, :] if lo is None else ap3[:, j:j + 1, lo:hi]
            return v.rearrange("p a c -> p (a c)")

        def vcol(vi, j):
            return vecs.ap()[:, vi:vi + 1, j:j + 1].rearrange("p a b -> p (a b)")

        def dcol(di, j):
            return dvec.ap()[:, di:di + 1, j:j + 1].rearrange("p a b -> p (a b)")

        def ccol(i, j):
            return carry.ap()[:, i:i + 1, j:j + 1].rearrange("p a b -> p (a b)")

        def crow(i):
            return carry.ap()[:, i:i + 1, :].transpose([0, 2, 1])

        def tapout(name, ap, wait):
            if name in tap_t and name not in tap_done:
                tap_done.add(name)
                tt = tap_t[name]
                tok = p.dma("sync", lambda e, ap=ap, tt=tt: e.dma_start(tt.ap(), ap),
                            waits=[wait], inc="out")
                p.tap_gate = tok
                return tok
            return None

        tL = {}
        tL["vecs"] = p.dma("sync", lambda e: e.dma_start(vecs[:], vecsH.ap()), inc="init")
        tL["b1"] = p.dma("sync", lambda e: e.dma_start(b1s[:], b1H.ap()), inc="init")
        tL["b2"] = p.dma("sync", lambda e: e.dma_start(b2s[:], b2H.ap()), inc="init")
        tL["wq"] = p.dma("sync", lambda e: e.dma_start(wqs[:], wqH.ap()), inc="init")
        tL["br"] = p.dma("sync", lambda e: e.dma_start(brs[:], brH.ap()), inc="init")
        tL["id"] = p.dma("sync", lambda e: e.dma_start(id128[:], id128H.ap()), inc="init")
        tL["on"] = p.dma("sync", lambda e: e.dma_start(onescol[:], onesH.ap()), inc="init")
        tL["or"] = p.dma("sync", lambda e: e.dma_start(onesrow[:], onesrowH.ap()), inc="init")
        tL["sel"] = p.dma("sync", lambda e: e.dma_start(sels[:], selH.ap()), inc="init")
        t_wp = [
            p.dma("gpsimd", lambda e: e.dma_start(wk_bf, wkH.ap()), inc="gwp"),
            p.dma("gpsimd", lambda e: e.dma_start(wv_bf, wvH.ap()), inc="gwp"),
            p.dma("gpsimd", lambda e: e.dma_start(wr_bf, wrH.ap()), inc="gwp"),
            p.dma("gpsimd", lambda e: e.dma_start(wo_bf, woH.ap()), inc="gwp"),
        ]
        t_init = p.op("vector", lambda e: e.memset(carry[:], 0.0), inc="vector")
        t_eps = p.op("vector", lambda e: e.memset(epsc[:], LN_EPS), inc="vector")
        p.op("vector", lambda e: e.tensor_add(dvec[:, D_UPW], vecs[:, V_U], vecs[:, V_W]),
             waits=[tL["vecs"]], inc="vector")
        p.op("vector", lambda e: e.tensor_sub(dvec[:, D_UMW], vecs[:, V_U], vecs[:, V_W]),
             inc="vector")
        t_dvec = p.op("vector", lambda e: e.tensor_scalar_mul(dvec[:, D_2W], vecs[:, V_W], 2.0),
                      inc="vector")
        t_onesb = p.op("vector", lambda e: e.tensor_copy(onescolb[:], onescol[:]),
                       waits=[tL["on"]], inc="vector")

        rpsf = rps.ap().rearrange("p j c -> p (j c)")
        bank = {"A": psA, "B": psB, "C": psC, "D": psD,
                "R0": rpsf[:, 0:512], "R1": rpsf[:, 512:1024],
                "R2": rpsf[:, 1024:1536], "R3": rpsf[:, 1536:2048]}
        blast = {}
        last_xch_use = None
        last_x1_use = None
        last_x1bf_use = None
        last_tgt2_use = None
        last_rps_use = None
        tok_moe_done = None
        w1_war = [[], []]
        w2_war = [[], []]
        cache_done = [None]

        for q in range(NQ):
            t_x1_ready = None
            t_x1bf_ready = None
            for c in range(NCH):
                t0 = q * TQ + c * C
                first = (t0 == 0)
                t_x = p.dma("sync", lambda e, t0=t0: e.dma_start(
                    xch[:, :, 1:C + 1], xH.ap()[:, :, t0:t0 + C]),
                    waits=[last_xch_use], inc="sx")
                t_c0 = p.op("vector", lambda e: e.tensor_copy(xch[:, :, 0:1], crow(0)),
                            waits=[t_x, t_init], inc="vector")
                t_xc = p.op("vector", lambda e: e.tensor_copy(crow(0), xch[:, :, C:C + 1]),
                            waits=[t_x, t_c0], inc="vector")

                t_xd = p.op("vector", lambda e: e.tensor_sub(
                    sA, xch[:, :, 1:C + 1], xch[:, :, 0:C]),
                    waits=[t_c0, tok_moe_done, ("gwc", 0)], inc="vector")
                t_mk = t_mv = t_mr = None
                for j in range(JD):
                    t_mk = p.op("vector", lambda e, j=j: e.scalar_tensor_tensor(
                        j2(kbf, j), j2(sA, j), vcol(V_TMK, j), j2(xch, j, 0, C),
                        ALU.mult, ALU.add), waits=[t_xd], inc="vector")
                for j in range(JD):
                    t_mv = p.op("vector", lambda e, j=j: e.scalar_tensor_tensor(
                        j2(vbf, j), j2(sA, j), vcol(V_TMV, j), j2(xch, j, 0, C),
                        ALU.mult, ALU.add), waits=[t_xd], inc="vector")
                for j in range(JD):
                    t_mr = p.op("vector", lambda e, j=j: e.scalar_tensor_tensor(
                        j2(sB, j), j2(sA, j), vcol(V_TMR, j), j2(xch, j, 0, C),
                        ALU.mult, ALU.add), waits=[t_xd], inc="vector")
                t_rsig = p.op("scalar", lambda e: e.activation(rbf, sB, AF.Sigmoid),
                              waits=[t_mr], inc="scalar")
                last_xch_use = t_mr

                tapout("key_bf", kbf, t_mk)
                tapout("val_bf", vbf, t_mv)
                tapout("rcp_bf", rbf, t_rsig)

                t_s = t_m = None
                for half in range(4):
                    bk = "A" if half % 2 == 0 else "B"
                    grp = [None, None]
                    for sub in range(2):
                        jE = half * 2 + sub
                        for jd in range(JD):
                            tok = p.op("tensor", lambda e, jE=jE, jd=jd, bk=bk, sub=sub: e.matmul(
                                bank[bk][:, sub * C:(sub + 1) * C],
                                wk_bf[:, jd, jE * 128:(jE + 1) * 128],
                                j2(kbf, jd), start=(jd == 0), stop=(jd == JD - 1)),
                                waits=([t_mk, t_wp[0], blast.get(bk)] if jd == 0 and sub == 0
                                       else ([t_mk] if jd == 0 else ())),
                                inc=("tensor" if jd == JD - 1 else None))
                            if jd == JD - 1:
                                grp[sub] = tok
                    for sub in range(2):
                        jE = half * 2 + sub
                        t_s = p.op("vector", lambda e, jE=jE, bk=bk, sub=sub: e.tensor_scalar(
                            j2(sA, jE), bank[bk][:, sub * C:(sub + 1) * C],
                            dcol(D_UPW, jE), dcol(D_2W, jE), ALU.add, ALU.max),
                            waits=[grp[1], t_dvec, t_mr], inc="vector")
                        t_m = p.op("vector", lambda e, jE=jE, bk=bk, sub=sub: e.tensor_scalar(
                            j2(sB, jE), bank[bk][:, sub * C:(sub + 1) * C],
                            dcol(D_UMW, jE), 0.0, ALU.add, ALU.min),
                            waits=[grp[1], t_rsig], inc="vector")
                    blast[bk] = t_m
                t_vev = None
                for half in range(4):
                    bk = "C" if half % 2 == 0 else "D"
                    grp = [None, None]
                    for sub in range(2):
                        jE = half * 2 + sub
                        for jd in range(JD):
                            tok = p.op("tensor", lambda e, jE=jE, jd=jd, bk=bk, sub=sub: e.matmul(
                                bank[bk][:, sub * C:(sub + 1) * C],
                                wv_bf[:, jd, jE * 128:(jE + 1) * 128],
                                j2(vbf, jd), start=(jd == 0), stop=(jd == JD - 1)),
                                waits=([t_mv, t_wp[1], blast.get(bk)] if jd == 0 and sub == 0
                                       else ([t_mv] if jd == 0 else ())),
                                inc=("tensor" if jd == JD - 1 else None))
                            if jd == JD - 1:
                                grp[sub] = tok
                    jE0 = half * 2
                    t_vev = p.op("scalar", lambda e, jE0=jE0, bk=bk: e.copy(
                        sV[:, jE0:jE0 + 2, :].rearrange("p a c -> p (a c)"), bank[bk][:]),
                        waits=[grp[1]], inc="scalar")
                    blast[bk] = t_vev
                t_rg = None
                for jE in range(E):
                    for jd in range(JD):
                        tok = p.op("tensor", lambda e, jE=jE, jd=jd: e.matmul(
                            j2(rps.ap(), jE), wr_bf[:, jd, jE * 128:(jE + 1) * 128],
                            j2(rbf, jd), start=(jd == 0), stop=(jd == JD - 1)),
                            waits=([t_rsig, t_wp[2], last_rps_use,
                                    blast.get("R0"), blast.get("R1"),
                                    blast.get("R2"), blast.get("R3")]
                                   if jd == 0 and jE == 0
                                   else ([t_rsig] if jd == 0 else ())),
                            inc=("tensor" if jd == JD - 1 else None))
                        if jd == JD - 1:
                            t_rg = tok

                tapout("s", sA, t_s)
                tapout("m", sB, t_m)

                t_Ls = None
                for j in range(JD):
                    init = 0.0 if first else ccol(1, j)
                    t_Ls = p.op("vector", lambda e, j=j, init=init: e.tensor_tensor_scan(
                        j2(sC, j), j2(sA, j), j2(sA, j), init, ALU.add, ALU.bypass),
                        waits=[t_s], inc="vector")
                t_W = p.op("vector", lambda e: e.tensor_add(sW, sB, sC),
                           waits=[t_Ls, t_m], inc="vector")
                t_Lc = p.op("vector", lambda e: e.tensor_copy(crow(1), sC[:, :, C - 1:C]),
                            waits=[t_W], inc="vector")
                t_z0 = p.op("vector", (lambda e: e.memset(sZ[:, :, 0:1], NEG_BIG)) if first
                            else (lambda e: e.tensor_copy(sZ[:, :, 0:1], crow(2))),
                            waits=[t_W], inc="vector")
                t_Zs = None
                for j in range(JD):
                    init = NEG_BIG if first else ccol(2, j)
                    t_Zs = p.op("vector", lambda e, j=j, init=init: e.tensor_tensor_scan(
                        j2(sZ, j, 1, C + 1), j2(sW, j), j2(sW, j), init, ALU.max, ALU.bypass),
                        waits=[t_W, t_z0], inc="vector")
                t_Zc = p.op("vector", lambda e: e.tensor_copy(crow(2), sZ[:, :, C:C + 1]),
                            waits=[t_Zs], inc="vector")
                t_d1 = p.op("vector", lambda e: e.tensor_sub(sA, sZ[:, :, 0:C], sZ[:, :, 1:C + 1]),
                            waits=[t_Zs], inc="vector")
                t_d2 = p.op("scalar", lambda e: e.activation(sA, sA, AF.Exp),
                            waits=[t_d1], inc="scalar")
                t_e1 = p.op("vector", lambda e: e.tensor_sub(sB, sW, sZ[:, :, 1:C + 1]),
                            waits=[t_Zs, t_d1], inc="vector")
                t_e2 = p.op("scalar", lambda e: e.activation(sB, sB, AF.Exp),
                            waits=[t_e1], inc="scalar")
                t_y = p.op("vector", lambda e: e.tensor_mul(sV, sB, sV),
                           waits=[t_e2, t_vev], inc="vector")
                tapout("Lpos", sC, t_Lc); tapout("W", sW, t_W)
                tapout("Z", sZ[:, :, 1:C + 1], t_Zs)
                tapout("delta", sA, t_d2); tapout("ew", sB, t_e2)
                t_ns = None
                for j in range(JD):
                    init = 0.0 if first else ccol(3, j)
                    t_ns = p.op("vector", lambda e, j=j, init=init: e.tensor_tensor_scan(
                        j2(sC, j), j2(sA, j), j2(sV, j), init, ALU.mult, ALU.add),
                        waits=[t_y, t_d2, t_Lc], inc="vector")
                t_ds = None
                for j in range(JD):
                    init = 0.0 if first else ccol(4, j)
                    t_ds = p.op("vector", lambda e, j=j, init=init: e.tensor_tensor_scan(
                        j2(sZ, j, 0, C), j2(sA, j), j2(sB, j), init, ALU.mult, ALU.add),
                        waits=[t_y, t_d2, t_Zc], inc="vector")
                t_nc = p.op("vector", lambda e: e.tensor_copy(crow(3), sC[:, :, C - 1:C]),
                            waits=[t_ns], inc="vector")
                t_dc = p.op("vector", lambda e: e.tensor_copy(crow(4), sZ[:, :, C - 1:C]),
                            waits=[t_ds], inc="vector")
                tapout("n", sC, t_nc); tapout("d", sZ[:, :, 0:C], t_dc)
                t_ln = p.op("scalar", lambda e: e.activation(sV, sZ[:, :, 0:C], AF.Ln),
                            waits=[t_ds, t_y], inc="scalar")
                t_rc = p.op("scalar", lambda e: e.activation(sV, sV, AF.Exp, scale=-1.0),
                            waits=[t_ln], inc="scalar")
                t_wkv = p.op("vector", lambda e: e.tensor_mul(sW, sC, sV),
                             waits=[t_rc, t_ns, t_e1], inc="vector")
                tapout("wkv", sW, t_wkv)
                t_wkvr = p.op("vector", lambda e: e.tensor_tensor(
                    wkvrbf[:, :, :] if False else wkvrbf, sW, rps.ap(), ALU.mult),
                    waits=[t_wkv, t_rg], inc="vector")
                last_rps_use = t_wkvr

                t_z0v = None
                for half in range(4):
                    bk = "A" if half % 2 == 0 else "B"
                    grp = [None, None]
                    for sub in range(2):
                        jd = half * 2 + sub
                        for jE in range(E):
                            tok = p.op("tensor", lambda e, jd=jd, jE=jE, bk=bk, sub=sub: e.matmul(
                                bank[bk][:, sub * C:(sub + 1) * C],
                                wo_bf[:, jE, jd * 128:(jd + 1) * 128],
                                j2(wkvrbf, jE), start=(jE == 0), stop=(jE == E - 1)),
                                waits=([t_wkvr, t_wp[3], blast.get(bk)] if jE == 0 and sub == 0
                                       else ([t_wkvr] if jE == 0 else ())),
                                inc=("tensor" if jE == E - 1 else None))
                            if jE == E - 1:
                                grp[sub] = tok
                    for sub in range(2):
                        jd = half * 2 + sub
                        t_z0v = p.op("vector", lambda e, jd=jd, bk=bk, sub=sub: e.scalar_tensor_tensor(
                            j2(sA, jd), bank[bk][:, sub * C:(sub + 1) * C],
                            vcol(V_BO, jd), j2(xch, jd, 1, C + 1), ALU.add, ALU.add),
                            waits=[grp[1], t_wkv], inc="vector")
                    blast[bk] = t_z0v
                last_xch_use = t_z0v
                tapout("z0", sA, t_z0v)

                t_zbf = p.op("vector", lambda e: e.tensor_copy(kbf, sA),
                             waits=[t_z0v], inc="vector")
                t_zsq = p.op("scalar", lambda e: e.activation(vbf, sA, AF.Square),
                             waits=[t_z0v], inc="scalar")
                g1 = None
                for j in range(JD):
                    g1 = p.op("tensor", lambda e, j=j: e.matmul(
                        psC[0:1, 0:C], onescolb[:], j2(kbf, j),
                        start=(j == 0), stop=(j == JD - 1)),
                        waits=([t_zbf, t_onesb, blast.get("C")] if j == 0 else ()),
                        inc=("tensor" if j == JD - 1 else None))
                g2 = None
                for j in range(JD):
                    g2 = p.op("tensor", lambda e, j=j: e.matmul(
                        psC[0:1, C:2 * C], onescolb[:], j2(vbf, j),
                        start=(j == 0), stop=(j == JD - 1)),
                        waits=([t_zsq] if j == 0 else ()),
                        inc=("tensor" if j == JD - 1 else None))
                rv = rows.ap()
                mu_r, ex2_r, rstd_r, mur_r = (rv[:, 0:C], rv[:, C:2 * C],
                                              rv[:, 2 * C:3 * C], rv[:, 3 * C:4 * C])
                t_mu = p.op("vector", lambda e: e.tensor_scalar_mul(mu_r, psC[0:1, 0:C], 1.0 / D),
                            waits=[g1, g2], inc="vector")
                t_ex = p.op("vector", lambda e: e.tensor_scalar_mul(ex2_r, psC[0:1, C:2 * C], 1.0 / D),
                            waits=[g2], inc="vector")
                blast["C"] = t_ex
                t_m2_ = p.op("vector", lambda e: e.tensor_mul(rstd_r, mu_r, mu_r),
                             waits=[t_mu], inc="vector")
                t_var = p.op("vector", lambda e: e.tensor_sub(rstd_r, ex2_r, rstd_r),
                             waits=[t_m2_, t_ex], inc="vector")
                t_lv = p.op("scalar", lambda e: e.activation(rstd_r, rstd_r, AF.Ln, bias=epsc[0:1, 0:1]),
                            waits=[t_var, t_eps], inc="scalar")
                t_rs = p.op("scalar", lambda e: e.activation(rstd_r, rstd_r, AF.Exp, scale=-0.5),
                            waits=[t_lv], inc="scalar")
                t_mr_ = p.op("vector", lambda e: e.tensor_mul(mur_r, mu_r, rstd_r),
                             waits=[t_rs], inc="vector")
                t_b1m = p.op("tensor", lambda e: e.matmul(
                    psD[:, 0:C], onesrow[:], rstd_r, start=True, stop=True),
                    waits=[t_rs, tL["or"], blast.get("D")], inc="tensor")
                t_b2m = p.op("tensor", lambda e: e.matmul(
                    psD[:, C:2 * C], onesrow[:], mur_r, start=True, stop=True),
                    waits=[t_mr_], inc="tensor")
                t_bc1 = p.op("vector", lambda e: e.tensor_copy(bct[:, 0], psD[:, 0:C]),
                             waits=[t_b1m, t_b2m], inc="vector")
                t_bc2 = p.op("vector", lambda e: e.tensor_copy(bct[:, 1], psD[:, C:2 * C]),
                             waits=[t_b2m], inc="vector")
                blast["D"] = t_bc2
                x1v = x1.ap()[:, :, c * C:(c + 1) * C]
                x1bv = x1bf.ap()[:, :, c * C:(c + 1) * C]
                t_x1a = p.op("vector", lambda e, x1v=x1v: e.tensor_tensor(
                    x1v, sA, bct.ap()[:, 0:1, :].broadcast_to([128, JD, C]), ALU.mult),
                    waits=[t_bc1, last_x1_use], inc="vector")
                t_x1b = p.op("vector", lambda e, x1v=x1v: e.tensor_tensor(
                    x1v, x1v, bct.ap()[:, 1:2, :].broadcast_to([128, JD, C]), ALU.subtract),
                    waits=[t_bc2, t_x1a], inc="vector")
                t_x1c = p.op("vector", lambda e, x1bv=x1bv, x1v=x1v: e.tensor_copy(x1bv, x1v),
                             waits=[t_x1b, last_x1bf_use], inc="vector")
                t_x1_ready, t_x1bf_ready = t_x1b, t_x1c

            tapout("x1", x1.ap(), t_x1_ready)

            glg = None
            for j in range(JD):
                glg = p.op("tensor", lambda e, j=j: e.matmul(
                    psD[0:E, 0:TQ], wqs[:, j, :], x1.ap()[:, j, :],
                    start=(j == 0), stop=(j == JD - 1)),
                    waits=([t_x1_ready, tL["wq"], blast.get("D")] if j == 0 else ()),
                    inc=("tensor" if j == JD - 1 else None))
            t_lg = p.op("vector", lambda e: e.tensor_scalar(
                logits_sb[:], psD[0:E, 0:TQ], brs[:, 0:1], None, ALU.add),
                waits=[glg, tL["br"]], inc="vector")
            blast["D"] = t_lg
            tapout("logits", logits_sb.ap(), t_lg)
            t_tr = None
            for tt in range(NT):
                t_tr = p.op("tensor", lambda e, tt=tt: e.transpose(
                    psC[:, tt * E:(tt + 1) * E], logits_sb[:, tt * 128:(tt + 1) * 128],
                    id128[0:E, 0:E]),
                    waits=([t_lg, tL["id"], blast.get("C")] if tt == 0 else [t_lg]),
                    inc="tensor")
            t_lgT = p.op("vector", lambda e: e.tensor_copy(
                lgT[:], psC[:, 0:NT * E].rearrange("p (t e) -> p t e", t=NT)),
                waits=[t_tr], inc="vector")
            blast["C"] = t_lgT
            t_rwT = []
            for tt in range(NT):
                lgv = lgT.ap()[:, tt, :]
                ind1 = bct.ap()[:, 0, 0:E]
                mskd = bct.ap()[:, 1, 0:E]
                t_m1 = p.op("vector", lambda e, lgv=lgv: e.reduce_max(
                    srow[:, 0:1], lgv, axis=AX.X), waits=[t_lgT], inc="vector")
                t_i1 = p.op("vector", lambda e, lgv=lgv, ind1=ind1: e.tensor_scalar(
                    ind1, lgv, srow[:, 0:1], None, ALU.is_ge), waits=[t_m1], inc="vector")
                t_mk_ = p.op("vector", lambda e, lgv=lgv, ind1=ind1, mskd=mskd: e.scalar_tensor_tensor(
                    mskd, ind1, NEG_BIG, lgv, ALU.mult, ALU.add), waits=[t_i1], inc="vector")
                t_m2 = p.op("vector", lambda e, mskd=mskd: e.reduce_max(
                    srow[:, 1:2], mskd, axis=AX.X), waits=[t_mk_], inc="vector")
                t_dm = p.op("vector", lambda e: e.tensor_sub(
                    srow[:, 2:3], srow[:, 0:1], srow[:, 1:2]), waits=[t_m2], inc="vector")
                t_w1 = p.op("scalar", lambda e: e.activation(
                    srow[:, 2:3], srow[:, 2:3], AF.Sigmoid), waits=[t_dm], inc="scalar")
                t_w2 = p.op("vector", lambda e: e.tensor_scalar(
                    srow[:, 3:4], srow[:, 2:3], 1.0, -1.0, ALU.subtract, ALU.mult),
                    waits=[t_w1], inc="vector")
                t_i2 = p.op("vector", lambda e, mskd=mskd: e.tensor_scalar(
                    mskd, mskd, srow[:, 1:2], None, ALU.is_ge), waits=[t_m2, t_w2], inc="vector")
                t_r1 = p.op("vector", lambda e, ind1=ind1: e.tensor_scalar(
                    ind1, ind1, srow[:, 2:3], None, ALU.mult), waits=[t_w1, t_mk_], inc="vector")
                t_r2 = p.op("vector", lambda e, lgv=lgv, ind1=ind1, mskd=mskd: e.scalar_tensor_tensor(
                    lgv, mskd, srow[:, 3:4], ind1, ALU.mult, ALU.add),
                    waits=[t_r1, t_i2], inc="vector")
                t_rwT.append(t_r2)
            t_tb = None
            for tt in range(NT):
                t_tb = p.op("tensor", lambda e, tt=tt: e.transpose(
                    psC[0:E, tt * 128:(tt + 1) * 128], lgT.ap()[:, tt, :], id128[:]),
                    waits=([t_rwT[tt], blast.get("C")] if tt == 0 else [t_rwT[tt]]),
                    inc="tensor")
            t_rwf = p.op("vector", lambda e: e.tensor_copy(rwfm[:], psC[0:E, 0:TQ]),
                         waits=[t_tb], inc="vector")
            blast["C"] = t_rwf
            tapout("rw", rwfm.ap(), t_rwf)

            arena_gate = t_x1bf_ready
            if q == 1:
                cache_done[0] = ("gwc", p.cnt["gwc"])
            t_w1dma = {}; t_w2dma = {}
            w1_buf_of = {}; w2_buf_of = {}
            nw1 = [0]; nw2 = [0]

            def ensure_w1(e_, s_):
                key = (e_, s_)
                if key not in t_w1dma:
                    buf = nw1[0] % 2; nw1[0] += 1
                    sem = "gw1a" if buf == 0 else "gw1b"
                    if q == 0:
                        tok = p.dma("gpsimd", lambda en, e_=e_, s_=s_, buf=buf: en.dma_start(
                            w1s[buf], w1H.ap()[e_, s_]),
                            waits=list(w1_war[buf]) + [arena_gate], inc=sem)
                        cw = p.dma("sync", lambda en, e_=e_, s_=s_, buf=buf: en.dma_start(
                            w1C.ap()[e_, s_], w1s[buf]), waits=[tok], inc="gwc")
                        w1_war[buf] = [cw]
                    else:
                        tok = p.dma("gpsimd", lambda en, e_=e_, s_=s_, buf=buf: en.dma_start(
                            w1s[buf], w1C.ap()[e_, s_]),
                            waits=list(w1_war[buf]) + [arena_gate, cache_done[0]],
                            inc=("cr1a" if buf == 0 else "cr1b"))
                        w1_war[buf] = []
                    t_w1dma[key] = tok
                    w1_buf_of[key] = buf
                return t_w1dma[key], w1_buf_of[key]

            def ensure_w2(e_, dc_):
                key = (e_, dc_)
                if key not in t_w2dma:
                    buf = nw2[0] % 2; nw2[0] += 1
                    sem = "gw2a" if buf == 0 else "gw2b"
                    if q == 0:
                        tok = p.dma("gpsimd", lambda en, e_=e_, dc_=dc_, buf=buf: en.dma_start(
                            w2s[buf], w2H.ap()[e_, dc_]),
                            waits=list(w2_war[buf]) + [arena_gate], inc=sem)
                        cw = p.dma("sync", lambda en, e_=e_, dc_=dc_, buf=buf: en.dma_start(
                            w2C.ap()[e_, dc_], w2s[buf]), waits=[tok], inc="gwc")
                        w2_war[buf] = [cw]
                    else:
                        tok = p.dma("gpsimd", lambda en, e_=e_, dc_=dc_, buf=buf: en.dma_start(
                            w2s[buf], w2C.ap()[e_, dc_]),
                            waits=list(w2_war[buf]) + [arena_gate, cache_done[0]],
                            inc=("cr2a" if buf == 0 else "cr2b"))
                        w2_war[buf] = []
                    t_w2dma[key] = tok
                    w2_buf_of[key] = buf
                return t_w2dma[key], w2_buf_of[key]

            t_rwbc_prev = None
            t_hev = None
            t_tg = None
            for e_ in range(E):
                for s_ in range(8):
                    tok_dma, buf = ensure_w1(e_, s_)
                    for fc in range(4):
                        jf = s_ * 4 + fc
                        bk = ["A", "B", "C", "D", "R0", "R1", "R2", "R3"][jf % 8]
                        grp = None
                        for jd in range(JD):
                            grp = p.op("tensor", lambda e, jd=jd, fc=fc, buf=buf, bk=bk: e.matmul(
                                bank[bk][:, 0:TQ], w1s[buf][:, jd, fc * 128:(fc + 1) * 128],
                                x1bf.ap()[:, jd, :], start=(jd == 0), stop=(jd == JD - 1)),
                                waits=([tok_dma, t_x1bf_ready, blast.get(bk)]
                                       + ([last_rps_use] if bk[0] == "R" else [])
                                       if jd == 0 else ()),
                                inc=("tensor" if jd == JD - 1 else None))
                        t_hev = p.op("scalar", lambda e, jf=jf, e_=e_, bk=bk: e.activation(
                            hsb[:, jf, :], bank[bk][:, 0:TQ], AF.Relu,
                            bias=b1s.ap()[:, e_:e_ + 1, jf:jf + 1].rearrange("p a b -> p (a b)")),
                            waits=[grp, tL["b1"]], inc="scalar")
                        blast[bk] = t_hev
                        w1_war[buf].append(grp)
                    nxt = (e_, s_ + 1) if s_ + 1 < 8 else ((e_ + 1, 0) if e_ + 1 < E else None)
                    if nxt is not None:
                        ensure_w1(*nxt)
                if e_ == 0:
                    tapout("h0", hsb, t_hev)
                t_h_done = t_hev
                t_rbm = p.op("tensor", lambda e, e_=e_: e.matmul(
                    psD[:, 0:TQ], sels[:, e_, :], rwfm[:], start=True, stop=True),
                    waits=[t_rwf, tL["sel"], blast.get("D")], inc="tensor")
                t_rwbc = p.op("vector", lambda e: e.tensor_copy(rwbc[:], psD[:, 0:TQ]),
                              waits=[t_rbm, t_rwbc_prev], inc="vector")
                blast["D"] = t_rwbc
                for dc in range(8):
                    tok_dma, buf = ensure_w2(e_, dc)
                    bk = ["R0", "R1", "R2", "R3", "A", "B", "C", "D"][dc % 8]
                    grp = None
                    for jf in range(JF):
                        grp = p.op("tensor", lambda e, jf=jf, buf=buf, bk=bk: e.matmul(
                            bank[bk][:, 0:TQ], w2s[buf][:, jf, :], hsb[:, jf, :],
                            start=(jf == 0), stop=(jf == JF - 1)),
                            waits=([tok_dma, t_h_done, blast.get(bk)] if jf == 0 else ()),
                            inc=("tensor" if jf == JF - 1 else None))
                    w2_war[buf].append(grp)
                    nxt = (e_, dc + 1) if dc + 1 < 8 else ((e_ + 1, 0) if e_ + 1 < E else None)
                    if nxt is not None:
                        ensure_w2(*nxt)
                    t_ot = p.op("vector", lambda e, bk=bk, e_=e_, dc=dc: e.scalar_tensor_tensor(
                        otmp[:], bank[bk][:, 0:TQ],
                        b2s.ap()[:, e_:e_ + 1, dc:dc + 1].rearrange("p a b -> p (a b)"),
                        rwbc[:], ALU.add, ALU.mult),
                        waits=[grp, t_rwbc, tL["b2"], t_tg], inc="vector")
                    tg = tgt2.ap()[:, dc, :]
                    if e_ == 0:
                        t_tg = p.op("vector", lambda e, tg=tg: e.tensor_copy(tg, otmp[:]),
                                    waits=[t_ot, last_tgt2_use], inc="vector")
                    else:
                        t_tg = p.op("vector", lambda e, tg=tg: e.tensor_add(tg, tg, otmp[:]),
                                    waits=[t_ot], inc="vector")
                    blast[bk] = t_tg
                t_rwbc_prev = t_tg
            t_tgt2_done = t_tg
            last_x1bf_use = t_hev
            tapout("tgt2", tgt2.ap(), t_tgt2_done)

            t_z2 = p.op("vector", lambda e: e.tensor_add(tgt2.ap(), x1.ap(), tgt2.ap()),
                        waits=[t_tgt2_done, t_x1_ready], inc="vector")

            def ln_pass(src_ap, t_src, apply_gb):
                zbf_t = hsb[:, 0:JD, :]
                zsq_t = hsb[:, JD:2 * JD, :]
                t_zb = p.op("vector", lambda e: e.tensor_copy(zbf_t, src_ap),
                            waits=[t_src], inc="vector")
                t_zq = p.op("scalar", lambda e: e.activation(zsq_t, src_ap, AF.Square),
                            waits=[t_src], inc="scalar")
                g1 = None
                for j in range(JD):
                    g1 = p.op("tensor", lambda e, j=j: e.matmul(
                        psA[0:1, 0:TQ], onescolb[:], zbf_t[:, j, :],
                        start=(j == 0), stop=(j == JD - 1)),
                        waits=([t_zb, blast.get("A")] if j == 0 else ()),
                        inc=("tensor" if j == JD - 1 else None))
                g2 = None
                for j in range(JD):
                    g2 = p.op("tensor", lambda e, j=j: e.matmul(
                        psB[0:1, 0:TQ], onescolb[:], zsq_t[:, j, :],
                        start=(j == 0), stop=(j == JD - 1)),
                        waits=([t_zq, blast.get("B")] if j == 0 else ()),
                        inc=("tensor" if j == JD - 1 else None))
                rv2 = rows.ap()
                mu_r = rv2[:, 0:TQ]
                ex_r = rv2[:, TQ:2 * TQ]
                sq_r = logits_sb.ap()[0:1, 0:TQ]
                t_mu = p.op("vector", lambda e: e.tensor_scalar_mul(mu_r, psA[0:1, 0:TQ], 1.0 / D),
                            waits=[g1], inc="vector")
                blast["A"] = t_mu
                t_ex = p.op("vector", lambda e: e.tensor_scalar_mul(ex_r, psB[0:1, 0:TQ], 1.0 / D),
                            waits=[g2], inc="vector")
                blast["B"] = t_ex
                t_q1 = p.op("vector", lambda e: e.tensor_mul(sq_r, mu_r, mu_r),
                            waits=[t_mu], inc="vector")
                t_q2 = p.op("vector", lambda e: e.tensor_sub(ex_r, ex_r, sq_r),
                            waits=[t_q1, t_ex], inc="vector")
                t_lv = p.op("scalar", lambda e: e.activation(ex_r, ex_r, AF.Ln, bias=epsc[0:1, 0:1]),
                            waits=[t_q2, t_eps], inc="scalar")
                t_rs = p.op("scalar", lambda e: e.activation(ex_r, ex_r, AF.Exp, scale=-0.5),
                            waits=[t_lv], inc="scalar")
                t_mr2 = p.op("vector", lambda e: e.tensor_mul(mu_r, mu_r, ex_r),
                             waits=[t_rs], inc="vector")
                t_bm1 = p.op("tensor", lambda e: e.matmul(
                    psC[:, 0:TQ], onesrow[:], ex_r, start=True, stop=True),
                    waits=[t_rs, blast.get("C")], inc="tensor")
                t_bm2 = p.op("tensor", lambda e: e.matmul(
                    psD[:, 0:TQ], onesrow[:], mu_r, start=True, stop=True),
                    waits=[t_mr2, blast.get("D")], inc="tensor")
                t_bc1 = p.op("vector", lambda e: e.tensor_copy(bcq[:, 0], psC[:, 0:TQ]),
                             waits=[t_bm1], inc="vector")
                blast["C"] = t_bc1
                t_bc2 = p.op("vector", lambda e: e.tensor_copy(bcq[:, 1], psD[:, 0:TQ]),
                             waits=[t_bm2], inc="vector")
                blast["D"] = t_bc2
                t_a1 = p.op("vector", lambda e: e.tensor_tensor(
                    src_ap, src_ap, bcq.ap()[:, 0:1, :].broadcast_to([128, JD, TQ]), ALU.mult),
                    waits=[t_bc1, t_zb, t_zq], inc="vector")
                t_a2 = p.op("vector", lambda e: e.tensor_tensor(
                    src_ap, src_ap, bcq.ap()[:, 1:2, :].broadcast_to([128, JD, TQ]), ALU.subtract),
                    waits=[t_bc2, t_a1], inc="vector")
                t_last = t_a2
                if apply_gb:
                    for j in range(JD):
                        t_last = p.op("vector", lambda e, j=j: e.tensor_scalar(
                            j2(src_ap, j), j2(src_ap, j), vcol(V_GAM, j), vcol(V_BET, j),
                            ALU.mult, ALU.add), waits=[t_a2], inc="vector")
                return t_last

            t_ln1 = ln_pass(tgt2.ap(), t_z2, apply_gb=True)
            t_z3 = p.op("vector", lambda e: e.tensor_add(tgt2.ap(), x1.ap(), tgt2.ap()),
                        waits=[t_ln1], inc="vector")
            last_x1_use = t_z3
            t_ln2 = ln_pass(tgt2.ap(), t_z3, apply_gb=False)
            t_out = p.dma("sync", lambda e, q=q: e.dma_start(
                outH.ap()[:, :, q * TQ:(q + 1) * TQ], tgt2.ap()),
                waits=[t_ln2], inc="out")
            last_tgt2_use = t_out
            tok_moe_done = t_ln2

        p.op("sync", None, waits=[(s, p.cnt[s]) for s in P.DMA_SEMS if p.cnt[s]])

        p.emit(blk)
    return nc, tap_t



def _prep_shared(inp):
    f32 = np.float32
    def A(k):
        return np.asarray(inp[k], f32)
    def projH(Wm):
        return np.ascontiguousarray(Wm.T.reshape(JD, 128, D).transpose(1, 0, 2))
    d = {}
    d["wkH"] = projH(A("L0_attn_Wk")); d["wvH"] = projH(A("L0_attn_Wv"))
    d["wrH"] = projH(A("L0_attn_Wr")); d["woH"] = projH(A("L0_attn_Wo"))
    d["w1H"] = np.ascontiguousarray(
        A("L0_ffn_W1").reshape(E, 8, 512, JD, 128).transpose(0, 1, 4, 3, 2))
    d["w2H"] = np.ascontiguousarray(
        A("L0_ffn_W2").reshape(E, 8, 128, JF, 128).transpose(0, 1, 4, 3, 2))
    vecs = np.stack([A("L0_attn_u"), A("L0_attn_w"), A("L0_attn_tmk"),
                     A("L0_attn_tmv"), A("L0_attn_tmr"), A("L0_attn_bo"),
                     A("L0_ffn_gamma"), A("L0_ffn_beta")], 0)
    d["vecsH"] = np.ascontiguousarray(vecs.reshape(8, JD, 128).transpose(2, 0, 1))
    d["b1H"] = np.ascontiguousarray(A("L0_ffn_b1").reshape(E, JF, 128).transpose(2, 0, 1))
    d["b2H"] = np.ascontiguousarray(A("L0_ffn_b2").reshape(E, E, 128).transpose(2, 0, 1))
    d["wqH"] = np.ascontiguousarray(A("L0_ffn_Wr").T.reshape(JD, 128, E).transpose(1, 0, 2))
    d["brH"] = np.ascontiguousarray(A("L0_ffn_br").reshape(E, 1))
    d["id128H"] = np.eye(128, dtype=f32)
    d["onesH"] = np.ones((128, 1), f32)
    d["onesrowH"] = np.ones((1, 128), f32)
    sel = np.zeros((E, E, 128), f32)
    for e_ in range(E):
        sel[e_, e_, :] = 1.0
    d["selH"] = sel
    return d


def kernel(**inputs):
    x = np.asarray(inputs["L0_attn_x"], np.float32)
    B, T, _ = x.shape
    shared = _prep_shared(inputs)
    nc, _ = build(T=T)
    in_maps = []
    for b in range(B):
        m = dict(shared)
        m["xH"] = np.ascontiguousarray(x[b].T.reshape(JD, 128, T).transpose(1, 0, 2))
        in_maps.append(m)
    res = run_bass_kernel_spmd(nc, in_maps, core_ids=list(range(8)))
    out = np.empty((B, T, D), np.float32)
    for b in range(B):
        oH = np.asarray(res.results[b]["outH"])
        out[b] = oH.transpose(1, 0, 2).reshape(D, T).T
    return out


# version 15
# speedup vs baseline: 1.2356x; 1.0313x over previous
import numpy as np
import concourse.bass as bass
import concourse.mybir as mybir
from concourse.bass_utils import run_bass_kernel_spmd

F32 = mybir.dt.float32
BF16 = mybir.dt.bfloat16
AF = mybir.ActivationFunctionType
ALU = mybir.AluOpType
AX = mybir.AxisListType

D = 1024
JD = 8
E = 8
F = 4096
JF = 32
LN_EPS = 1e-5
NEG_BIG = -1.0e30


class Op:
    __slots__ = ("eng", "fn", "waits", "tok")
    def __init__(self, eng, fn, waits, tok):
        self.eng = eng; self.fn = fn; self.waits = waits; self.tok = tok

class P:
    ENGS = ("sync", "gpsimd", "vector", "scalar", "tensor")
    DMA_SEMS = frozenset({"init", "sx", "gwp", "gw1a", "gw1b", "gw2a", "gw2b",
                          "cr1a", "cr1b", "cr2a", "cr2b", "gwc", "out"})

    def __init__(self, sems):
        self.sems = sems
        self.cnt = {k: 0 for k in sems}
        self.ops = []
        self.tap_gate = None

    def op(self, eng, fn=None, waits=(), inc=None, amt=None):
        w = [x for x in waits if x is not None]
        if self.tap_gate is not None:
            w.append(self.tap_gate)
        w = tuple((s, self.cnt[s]) if s in self.DMA_SEMS else (s, v) for (s, v) in w)
        tok = None
        if inc is not None:
            a = amt if amt is not None else 1
            self.cnt[inc] += a
            tok = (inc, self.cnt[inc], a)
        self.ops.append(Op(eng, fn, w, tok))
        return None if tok is None else (tok[0], tok[1])

    def dma(self, eng, fn, waits=(), inc=None):
        return self.op(eng, fn, waits=waits, inc=(inc or eng), amt=16)

    def emit(self, block):
        for eng in self.ENGS:
            getattr(block, eng)(self._mk_emitter(eng))

    def _mk_emitter(self, eng):
        def run(e):
            waited = {}
            for o in self.ops:
                if o.eng != eng:
                    continue
                for (s, v) in o.waits:
                    if waited.get(s, -1) >= v:
                        continue
                    e.wait_ge(self.sems[s], v)
                    waited[s] = v
                if o.fn is None:
                    continue
                inst = o.fn(e)
                if o.tok is not None:
                    inst.then_inc(self.sems[o.tok[0]], o.tok[2])
        return run



def build(T=4096, TQ=512, C=256, taps=()):
    taps = set(taps)
    NQ = T // TQ
    NCH = TQ // C
    NT = TQ // 128
    nc = bass.Bass("TRN2", target_bir_lowering=False)

    xH = nc.dram_tensor("xH", [128, JD, T], F32, kind="ExternalInput")
    wkH = nc.dram_tensor("wkH", [128, JD, D], F32, kind="ExternalInput")
    wvH = nc.dram_tensor("wvH", [128, JD, D], F32, kind="ExternalInput")
    wrH = nc.dram_tensor("wrH", [128, JD, D], F32, kind="ExternalInput")
    woH = nc.dram_tensor("woH", [128, JD, D], F32, kind="ExternalInput")
    w1H = nc.dram_tensor("w1H", [E, 8, 128, JD, 512], F32, kind="ExternalInput")
    w2H = nc.dram_tensor("w2H", [E, 8, 128, JF, 128], F32, kind="ExternalInput")
    vecsH = nc.dram_tensor("vecsH", [128, 8, JD], F32, kind="ExternalInput")
    b1H = nc.dram_tensor("b1H", [128, E, JF], F32, kind="ExternalInput")
    b2H = nc.dram_tensor("b2H", [128, E, E], F32, kind="ExternalInput")
    wqH = nc.dram_tensor("wqH", [128, JD, E], F32, kind="ExternalInput")
    brH = nc.dram_tensor("brH", [E, 1], F32, kind="ExternalInput")
    id128H = nc.dram_tensor("id128H", [128, 128], F32, kind="ExternalInput")
    onesH = nc.dram_tensor("onesH", [128, 1], F32, kind="ExternalInput")
    onesrowH = nc.dram_tensor("onesrowH", [1, 128], F32, kind="ExternalInput")
    selH = nc.dram_tensor("selH", [E, E, 128], F32, kind="ExternalInput")
    outH = nc.dram_tensor("outH", [128, JD, T], F32, kind="ExternalOutput")
    w1C = nc.dram_tensor("w1C", [E, 8, 128, JD, 512], BF16)
    w2C = nc.dram_tensor("w2C", [E, 8, 128, JF, 128], BF16)

    tap_t = {}
    tap_done = set()
    def mktap(name, shape, dtype=F32):
        if name in taps:
            tap_t[name] = nc.dram_tensor("tap_" + name, shape, dtype, kind="ExternalOutput")

    mktap("key_bf", [128, JD, C], BF16); mktap("val_bf", [128, JD, C], BF16)
    mktap("rcp_bf", [128, JD, C], BF16)
    mktap("s", [128, JD, C]); mktap("m", [128, JD, C])
    mktap("Lpos", [128, JD, C]); mktap("W", [128, JD, C]); mktap("Z", [128, JD, C])
    mktap("delta", [128, JD, C]); mktap("ew", [128, JD, C])
    mktap("n", [128, JD, C]); mktap("d", [128, JD, C]); mktap("wkv", [128, JD, C])
    mktap("z0", [128, JD, C]); mktap("x1", [128, JD, TQ])
    mktap("logits", [E, TQ]); mktap("rw", [E, TQ])
    mktap("h0", [128, JF, TQ], BF16); mktap("tgt2", [128, JD, TQ])

    SCN = JD * C
    ARENA = 8 * (SCN + JD * (C + 1)) // 8
    oA, oB, oC = 0, SCN, 2 * SCN
    oZ = 3 * SCN
    oW = oZ + JD * (C + 1)
    oV = oW + SCN
    oKVR = oV + SCN
    oWKVR = oKVR + 3 * SCN // 2
    attn_end = oWKVR + SCN // 2
    oH = 0
    oW1 = oH + JF * TQ // 2
    oW2 = oW1 + 2 * (JD * 256)
    moe_end = oW2 + 2 * (JF * 64)
    ARENA = max(attn_end, moe_end)

    from contextlib import ExitStack
    with ExitStack() as stk:
        en = stk.enter_context
        x1 = en(nc.sbuf_tensor("x1", [128, JD, TQ], F32))
        x1bf = en(nc.sbuf_tensor("x1bf", [128, JD, TQ], BF16))
        tgt2 = en(nc.sbuf_tensor("tgt2", [128, JD, TQ], F32))
        xch = en(nc.sbuf_tensor("xch", [128, JD, C + 1], F32))
        wproj = en(nc.sbuf_tensor("wproj", [128, 4 * JD, D], BF16))
        vecs = en(nc.sbuf_tensor("vecs", [128, 8, JD], F32))
        dvec = en(nc.sbuf_tensor("dvec", [128, 3, JD], F32))
        b1s = en(nc.sbuf_tensor("b1s", [128, E, JF], F32))
        b2s = en(nc.sbuf_tensor("b2s", [128, E, E], F32))
        wqs = en(nc.sbuf_tensor("wqs", [128, JD, E], F32))
        brs = en(nc.sbuf_tensor("brs", [E, 1], F32))
        id128 = en(nc.sbuf_tensor("id128", [128, 128], F32))
        onescol = en(nc.sbuf_tensor("onescol", [128, 1], F32))
        onescolb = en(nc.sbuf_tensor("onescolb", [128, 1], BF16))
        onesrow = en(nc.sbuf_tensor("onesrow", [1, 128], F32))
        sels = en(nc.sbuf_tensor("sels", [E, E, 128], F32))
        carry = en(nc.sbuf_tensor("carry", [128, 5, JD], F32))
        bct = en(nc.sbuf_tensor("bct", [128, 2, C], F32))
        bcq = en(nc.sbuf_tensor("bcq", [128, 2, TQ], F32))
        rwbc = en(nc.sbuf_tensor("rwbc", [128, TQ], F32))
        otmp = en(nc.sbuf_tensor("otmp", [128, TQ], F32))
        logits_sb = en(nc.sbuf_tensor("logits", [E, TQ], F32))
        rwfm = en(nc.sbuf_tensor("rwfm", [E, TQ], F32))
        lgT = en(nc.sbuf_tensor("lgT", [128, NT, E], F32))
        rows = en(nc.sbuf_tensor("rows", [1, 4 * max(C, TQ // 2)], F32))
        srow = en(nc.sbuf_tensor("srow", [128, 8], F32))
        epsc = en(nc.sbuf_tensor("epsc", [128, 1], F32))
        arena = en(nc.sbuf_tensor("arena", [128, ARENA], F32))
        psA = en(nc.psum_tensor("psA", [128, 512], F32))
        psB = en(nc.psum_tensor("psB", [128, 512], F32))
        psC = en(nc.psum_tensor("psC", [128, 512], F32))
        psD = en(nc.psum_tensor("psD", [128, 512], F32))
        rps = en(nc.psum_tensor("rps", [128, JD, C], F32))
        S0 = en(nc.semaphore("sm_sync"))
        S1 = en(nc.semaphore("sm_gpsimd"))
        S2 = en(nc.semaphore("sm_vector"))
        S3 = en(nc.semaphore("sm_scalar"))
        S4 = en(nc.semaphore("sm_tensor"))
        S5 = en(nc.semaphore("sm_out"))
        S6 = en(nc.semaphore("sm_gwp"))
        S7 = en(nc.semaphore("sm_gw1a"))
        S8 = en(nc.semaphore("sm_gw1b"))
        S9 = en(nc.semaphore("sm_gw2a"))
        S10 = en(nc.semaphore("sm_gw2b"))
        S11 = en(nc.semaphore("sm_init"))
        S12 = en(nc.semaphore("sm_sx"))
        S13 = en(nc.semaphore("sm_gwc"))
        S14 = en(nc.semaphore("sm_cr1a"))
        S15 = en(nc.semaphore("sm_cr1b"))
        S16 = en(nc.semaphore("sm_cr2a"))
        S17 = en(nc.semaphore("sm_cr2b"))
        blk = en(nc.Block())
        sems = {"sync": S0, "gpsimd": S1, "vector": S2, "scalar": S3,
                "tensor": S4, "out": S5, "gwp": S6, "gw1a": S7, "gw1b": S8,
                "gw2a": S9, "gw2b": S10, "init": S11, "sx": S12, "gwc": S13,
                "cr1a": S14, "cr1b": S15, "cr2a": S16, "cr2b": S17}
        p = P(sems)

        def carve(off, *shape, dt=F32):
            n = int(np.prod(shape))
            if dt == BF16:
                ap = arena.ap()[:, off:off + (n + 1) // 2].bitcast(BF16)
            else:
                ap = arena.ap()[:, off:off + n]
            if len(shape) > 1:
                dims = " ".join(f"d{i}" for i in range(len(shape)))
                kw = {f"d{i}": s_ for i, s_ in enumerate(shape)}
                ap = ap.rearrange(f"p ({dims}) -> p {dims}", **kw)
            return ap

        sA = carve(oA, JD, C); sB = carve(oB, JD, C); sC = carve(oC, JD, C)
        sZ = carve(oZ, JD, C + 1)
        sW = carve(oW, JD, C); sV = carve(oV, JD, C)
        kbf = carve(oKVR, JD, C, dt=BF16)
        vbf = carve(oKVR + SCN // 2, JD, C, dt=BF16)
        rbf = carve(oKVR + SCN, JD, C, dt=BF16)
        wkvrbf = carve(oWKVR, JD, C, dt=BF16)
        hsb = carve(oH, JF, TQ, dt=BF16)
        w1s = [carve(oW1 + i * JD * 256, JD, 512, dt=BF16) for i in range(2)]
        w2s = [carve(oW2 + i * JF * 64, JF, 128, dt=BF16) for i in range(2)]

        wpv = wproj.ap().rearrange("p (m j) d -> p m j d", m=4)
        wk_bf, wv_bf, wr_bf, wo_bf = wpv[:, 0], wpv[:, 1], wpv[:, 2], wpv[:, 3]

        V_U, V_W, V_TMK, V_TMV, V_TMR, V_BO, V_GAM, V_BET = range(8)
        D_UPW, D_UMW, D_2W = range(3)

        def j2(ap3, j, lo=None, hi=None):
            v = ap3[:, j:j + 1, :] if lo is None else ap3[:, j:j + 1, lo:hi]
            return v.rearrange("p a c -> p (a c)")

        def vcol(vi, j):
            return vecs.ap()[:, vi:vi + 1, j:j + 1].rearrange("p a b -> p (a b)")

        def dcol(di, j):
            return dvec.ap()[:, di:di + 1, j:j + 1].rearrange("p a b -> p (a b)")

        def ccol(i, j):
            return carry.ap()[:, i:i + 1, j:j + 1].rearrange("p a b -> p (a b)")

        def crow(i):
            return carry.ap()[:, i:i + 1, :].transpose([0, 2, 1])

        def tapout(name, ap, wait):
            if name in tap_t and name not in tap_done:
                tap_done.add(name)
                tt = tap_t[name]
                tok = p.dma("sync", lambda e, ap=ap, tt=tt: e.dma_start(tt.ap(), ap),
                            waits=[wait], inc="out")
                p.tap_gate = tok
                return tok
            return None

        tL = {}
        tL["vecs"] = p.dma("sync", lambda e: e.dma_start(vecs[:], vecsH.ap()), inc="init")
        tL["b1"] = p.dma("sync", lambda e: e.dma_start(b1s[:], b1H.ap()), inc="init")
        tL["b2"] = p.dma("sync", lambda e: e.dma_start(b2s[:], b2H.ap()), inc="init")
        tL["wq"] = p.dma("sync", lambda e: e.dma_start(wqs[:], wqH.ap()), inc="init")
        tL["br"] = p.dma("sync", lambda e: e.dma_start(brs[:], brH.ap()), inc="init")
        tL["id"] = p.dma("sync", lambda e: e.dma_start(id128[:], id128H.ap()), inc="init")
        tL["on"] = p.dma("sync", lambda e: e.dma_start(onescol[:], onesH.ap()), inc="init")
        tL["or"] = p.dma("sync", lambda e: e.dma_start(onesrow[:], onesrowH.ap()), inc="init")
        tL["sel"] = p.dma("sync", lambda e: e.dma_start(sels[:], selH.ap()), inc="init")
        t_wp = [
            p.dma("gpsimd", lambda e: e.dma_start(wk_bf, wkH.ap()), inc="gwp"),
            p.dma("gpsimd", lambda e: e.dma_start(wv_bf, wvH.ap()), inc="gwp"),
            p.dma("gpsimd", lambda e: e.dma_start(wr_bf, wrH.ap()), inc="gwp"),
            p.dma("gpsimd", lambda e: e.dma_start(wo_bf, woH.ap()), inc="gwp"),
        ]
        t_init = p.op("vector", lambda e: e.memset(carry[:], 0.0), inc="vector")
        t_eps = p.op("vector", lambda e: e.memset(epsc[:], LN_EPS), inc="vector")
        p.op("vector", lambda e: e.tensor_add(dvec[:, D_UPW], vecs[:, V_U], vecs[:, V_W]),
             waits=[tL["vecs"]], inc="vector")
        p.op("vector", lambda e: e.tensor_sub(dvec[:, D_UMW], vecs[:, V_U], vecs[:, V_W]),
             inc="vector")
        t_dvec = p.op("vector", lambda e: e.tensor_scalar_mul(dvec[:, D_2W], vecs[:, V_W], 2.0),
                      inc="vector")
        t_onesb = p.op("vector", lambda e: e.tensor_copy(onescolb[:], onescol[:]),
                       waits=[tL["on"]], inc="vector")

        rpsf = rps.ap().rearrange("p j c -> p (j c)")
        bank = {"A": psA, "B": psB, "C": psC, "D": psD,
                "R0": rpsf[:, 0:512], "R1": rpsf[:, 512:1024],
                "R2": rpsf[:, 1024:1536], "R3": rpsf[:, 1536:2048]}
        blast = {}
        last_xch_use = None
        last_x1_use = None
        last_x1bf_use = None
        last_tgt2_use = None
        last_rps_use = None
        tok_moe_done = None
        tok_moe_done2 = None
        w1_war = [[], []]
        w2_war = [[], []]
        cache_done = [None]

        for q in range(NQ):
            t_x1_ready = None
            t_x1bf_ready = None
            for c in range(NCH):
                t0 = q * TQ + c * C
                first = (t0 == 0)
                t_x = p.dma("sync", lambda e, t0=t0: e.dma_start(
                    xch[:, :, 1:C + 1], xH.ap()[:, :, t0:t0 + C]),
                    waits=[last_xch_use], inc="sx")
                t_c0 = p.op("vector", lambda e: e.tensor_copy(xch[:, :, 0:1], crow(0)),
                            waits=[t_x, t_init], inc="vector")
                t_xc = p.op("vector", lambda e: e.tensor_copy(crow(0), xch[:, :, C:C + 1]),
                            waits=[t_x, t_c0], inc="vector")

                t_xd = p.op("vector", lambda e: e.tensor_sub(
                    sA, xch[:, :, 1:C + 1], xch[:, :, 0:C]),
                    waits=[t_c0, tok_moe_done, tok_moe_done2, ("gwc", 0)], inc="vector")
                t_mk = t_mv = t_mr = None
                for j in range(JD):
                    t_mk = p.op("vector", lambda e, j=j: e.scalar_tensor_tensor(
                        j2(kbf, j), j2(sA, j), vcol(V_TMK, j), j2(xch, j, 0, C),
                        ALU.mult, ALU.add), waits=[t_xd], inc="vector")
                for j in range(JD):
                    t_mv = p.op("vector", lambda e, j=j: e.scalar_tensor_tensor(
                        j2(vbf, j), j2(sA, j), vcol(V_TMV, j), j2(xch, j, 0, C),
                        ALU.mult, ALU.add), waits=[t_xd], inc="vector")
                for j in range(JD):
                    t_mr = p.op("vector", lambda e, j=j: e.scalar_tensor_tensor(
                        j2(sB, j), j2(sA, j), vcol(V_TMR, j), j2(xch, j, 0, C),
                        ALU.mult, ALU.add), waits=[t_xd], inc="vector")
                t_rsig = p.op("scalar", lambda e: e.activation(rbf, sB, AF.Sigmoid),
                              waits=[t_mr], inc="scalar")
                last_xch_use = t_mr

                tapout("key_bf", kbf, t_mk)
                tapout("val_bf", vbf, t_mv)
                tapout("rcp_bf", rbf, t_rsig)

                t_s = t_m = None
                for half in range(4):
                    bk = "A" if half % 2 == 0 else "B"
                    grp = [None, None]
                    for sub in range(2):
                        jE = half * 2 + sub
                        for jd in range(JD):
                            tok = p.op("tensor", lambda e, jE=jE, jd=jd, bk=bk, sub=sub: e.matmul(
                                bank[bk][:, sub * C:(sub + 1) * C],
                                wk_bf[:, jd, jE * 128:(jE + 1) * 128],
                                j2(kbf, jd), start=(jd == 0), stop=(jd == JD - 1)),
                                waits=([t_mk, t_wp[0], blast.get(bk)] if jd == 0 and sub == 0
                                       else ([t_mk] if jd == 0 else ())),
                                inc=("tensor" if jd == JD - 1 else None))
                            if jd == JD - 1:
                                grp[sub] = tok
                    for sub in range(2):
                        jE = half * 2 + sub
                        t_s = p.op("vector", lambda e, jE=jE, bk=bk, sub=sub: e.tensor_scalar(
                            j2(sA, jE), bank[bk][:, sub * C:(sub + 1) * C],
                            dcol(D_UPW, jE), dcol(D_2W, jE), ALU.add, ALU.max),
                            waits=[grp[1], t_dvec, t_mr], inc="vector")
                        t_m = p.op("vector", lambda e, jE=jE, bk=bk, sub=sub: e.tensor_scalar(
                            j2(sB, jE), bank[bk][:, sub * C:(sub + 1) * C],
                            dcol(D_UMW, jE), 0.0, ALU.add, ALU.min),
                            waits=[grp[1], t_rsig], inc="vector")
                    blast[bk] = t_m
                t_vev = None
                for half in range(4):
                    bk = "C" if half % 2 == 0 else "D"
                    grp = [None, None]
                    for sub in range(2):
                        jE = half * 2 + sub
                        for jd in range(JD):
                            tok = p.op("tensor", lambda e, jE=jE, jd=jd, bk=bk, sub=sub: e.matmul(
                                bank[bk][:, sub * C:(sub + 1) * C],
                                wv_bf[:, jd, jE * 128:(jE + 1) * 128],
                                j2(vbf, jd), start=(jd == 0), stop=(jd == JD - 1)),
                                waits=([t_mv, t_wp[1], blast.get(bk)] if jd == 0 and sub == 0
                                       else ([t_mv] if jd == 0 else ())),
                                inc=("tensor" if jd == JD - 1 else None))
                            if jd == JD - 1:
                                grp[sub] = tok
                    jE0 = half * 2
                    t_vev = p.op("scalar", lambda e, jE0=jE0, bk=bk: e.copy(
                        sV[:, jE0:jE0 + 2, :].rearrange("p a c -> p (a c)"), bank[bk][:]),
                        waits=[grp[1]], inc="scalar")
                    blast[bk] = t_vev
                t_rg = None
                for jE in range(E):
                    for jd in range(JD):
                        tok = p.op("tensor", lambda e, jE=jE, jd=jd: e.matmul(
                            j2(rps.ap(), jE), wr_bf[:, jd, jE * 128:(jE + 1) * 128],
                            j2(rbf, jd), start=(jd == 0), stop=(jd == JD - 1)),
                            waits=([t_rsig, t_wp[2], last_rps_use,
                                    blast.get("R0"), blast.get("R1"),
                                    blast.get("R2"), blast.get("R3")]
                                   if jd == 0 and jE == 0
                                   else ([t_rsig] if jd == 0 else ())),
                            inc=("tensor" if jd == JD - 1 else None))
                        if jd == JD - 1:
                            t_rg = tok

                tapout("s", sA, t_s)
                tapout("m", sB, t_m)

                t_Ls = None
                for j in range(JD):
                    init = 0.0 if first else ccol(1, j)
                    t_Ls = p.op("vector", lambda e, j=j, init=init: e.tensor_tensor_scan(
                        j2(sC, j), j2(sA, j), j2(sA, j), init, ALU.add, ALU.bypass),
                        waits=[t_s], inc="vector")
                t_W = p.op("vector", lambda e: e.tensor_add(sW, sB, sC),
                           waits=[t_Ls, t_m], inc="vector")
                def hb(tok):
                    p.op("tensor", lambda e: e.matmul(
                        psC[0:1, 0:1], onescolb[:], onescolb[:],
                        start=True, stop=True), waits=[tok, t_vev])
                hb(t_Ls)
                t_Lc = p.op("vector", lambda e: e.tensor_copy(crow(1), sC[:, :, C - 1:C]),
                            waits=[t_W], inc="vector")
                t_z0 = p.op("vector", (lambda e: e.memset(sZ[:, :, 0:1], NEG_BIG)) if first
                            else (lambda e: e.tensor_copy(sZ[:, :, 0:1], crow(2))),
                            waits=[t_W], inc="vector")
                t_Zs = None
                for j in range(JD):
                    init = NEG_BIG if first else ccol(2, j)
                    t_Zs = p.op("vector", lambda e, j=j, init=init: e.tensor_tensor_scan(
                        j2(sZ, j, 1, C + 1), j2(sW, j), j2(sW, j), init, ALU.max, ALU.bypass),
                        waits=[t_W, t_z0], inc="vector")
                hb(t_Zs)
                t_Zc = p.op("vector", lambda e: e.tensor_copy(crow(2), sZ[:, :, C:C + 1]),
                            waits=[t_Zs], inc="vector")
                t_d1 = p.op("vector", lambda e: e.tensor_sub(sA, sZ[:, :, 0:C], sZ[:, :, 1:C + 1]),
                            waits=[t_Zs], inc="vector")
                t_d2 = p.op("scalar", lambda e: e.activation(sA, sA, AF.Exp),
                            waits=[t_d1], inc="scalar")
                t_e1 = p.op("vector", lambda e: e.tensor_sub(sB, sW, sZ[:, :, 1:C + 1]),
                            waits=[t_Zs, t_d1], inc="vector")
                t_e2 = p.op("scalar", lambda e: e.activation(sB, sB, AF.Exp),
                            waits=[t_e1], inc="scalar")
                hb(t_e2)
                t_y = p.op("vector", lambda e: e.tensor_mul(sV, sB, sV),
                           waits=[t_e2, t_vev], inc="vector")
                tapout("Lpos", sC, t_Lc); tapout("W", sW, t_W)
                tapout("Z", sZ[:, :, 1:C + 1], t_Zs)
                tapout("delta", sA, t_d2); tapout("ew", sB, t_e2)
                t_ns = None
                for j in range(JD):
                    init = 0.0 if first else ccol(3, j)
                    t_ns = p.op("vector", lambda e, j=j, init=init: e.tensor_tensor_scan(
                        j2(sC, j), j2(sA, j), j2(sV, j), init, ALU.mult, ALU.add),
                        waits=[t_y, t_d2, t_Lc], inc="vector")
                t_ds = None
                for j in range(JD):
                    init = 0.0 if first else ccol(4, j)
                    t_ds = p.op("vector", lambda e, j=j, init=init: e.tensor_tensor_scan(
                        j2(sZ, j, 0, C), j2(sA, j), j2(sB, j), init, ALU.mult, ALU.add),
                        waits=[t_y, t_d2, t_Zc], inc="vector")
                hb(t_ns)
                t_nc = p.op("vector", lambda e: e.tensor_copy(crow(3), sC[:, :, C - 1:C]),
                            waits=[t_ns], inc="vector")
                hb(t_ds)
                t_dc = p.op("vector", lambda e: e.tensor_copy(crow(4), sZ[:, :, C - 1:C]),
                            waits=[t_ds], inc="vector")
                tapout("n", sC, t_nc); tapout("d", sZ[:, :, 0:C], t_dc)
                t_ln = p.op("scalar", lambda e: e.activation(sV, sZ[:, :, 0:C], AF.Ln),
                            waits=[t_ds, t_y], inc="scalar")
                t_rc = p.op("scalar", lambda e: e.activation(sV, sV, AF.Exp, scale=-1.0),
                            waits=[t_ln], inc="scalar")
                t_wkv = p.op("vector", lambda e: e.tensor_mul(sW, sC, sV),
                             waits=[t_rc, t_ns, t_e1], inc="vector")
                tapout("wkv", sW, t_wkv)
                hb(t_wkv)
                t_wkvr = p.op("vector", lambda e: e.tensor_tensor(
                    wkvrbf[:, :, :] if False else wkvrbf, sW, rps.ap(), ALU.mult),
                    waits=[t_wkv, t_rg], inc="vector")
                last_rps_use = t_wkvr

                t_z0v = None
                for half in range(4):
                    bk = "A" if half % 2 == 0 else "B"
                    grp = [None, None]
                    for sub in range(2):
                        jd = half * 2 + sub
                        for jE in range(E):
                            tok = p.op("tensor", lambda e, jd=jd, jE=jE, bk=bk, sub=sub: e.matmul(
                                bank[bk][:, sub * C:(sub + 1) * C],
                                wo_bf[:, jE, jd * 128:(jd + 1) * 128],
                                j2(wkvrbf, jE), start=(jE == 0), stop=(jE == E - 1)),
                                waits=([t_wkvr, t_wp[3], blast.get(bk)] if jE == 0 and sub == 0
                                       else ([t_wkvr] if jE == 0 else ())),
                                inc=("tensor" if jE == E - 1 else None))
                            if jE == E - 1:
                                grp[sub] = tok
                    for sub in range(2):
                        jd = half * 2 + sub
                        t_z0v = p.op("vector", lambda e, jd=jd, bk=bk, sub=sub: e.scalar_tensor_tensor(
                            j2(sA, jd), bank[bk][:, sub * C:(sub + 1) * C],
                            vcol(V_BO, jd), j2(xch, jd, 1, C + 1), ALU.add, ALU.add),
                            waits=[grp[1], t_wkv], inc="vector")
                    blast[bk] = t_z0v
                last_xch_use = t_z0v
                tapout("z0", sA, t_z0v)

                t_zbf = p.op("vector", lambda e: e.tensor_copy(kbf, sA),
                             waits=[t_z0v], inc="vector")
                t_zsq = p.op("scalar", lambda e: e.activation(vbf, sA, AF.Square),
                             waits=[t_z0v], inc="scalar")
                g1 = None
                for j in range(JD):
                    g1 = p.op("tensor", lambda e, j=j: e.matmul(
                        psC[0:1, 0:C], onescolb[:], j2(kbf, j),
                        start=(j == 0), stop=(j == JD - 1)),
                        waits=([t_zbf, t_onesb, blast.get("C")] if j == 0 else ()),
                        inc=("tensor" if j == JD - 1 else None))
                g2 = None
                for j in range(JD):
                    g2 = p.op("tensor", lambda e, j=j: e.matmul(
                        psC[0:1, C:2 * C], onescolb[:], j2(vbf, j),
                        start=(j == 0), stop=(j == JD - 1)),
                        waits=([t_zsq] if j == 0 else ()),
                        inc=("tensor" if j == JD - 1 else None))
                rv = rows.ap()
                mu_r, ex2_r, rstd_r, mur_r = (rv[:, 0:C], rv[:, C:2 * C],
                                              rv[:, 2 * C:3 * C], rv[:, 3 * C:4 * C])
                t_mu = p.op("vector", lambda e: e.tensor_scalar_mul(mu_r, psC[0:1, 0:C], 1.0 / D),
                            waits=[g1, g2], inc="vector")
                t_ex = p.op("vector", lambda e: e.tensor_scalar_mul(ex2_r, psC[0:1, C:2 * C], 1.0 / D),
                            waits=[g2], inc="vector")
                blast["C"] = t_ex
                t_m2_ = p.op("vector", lambda e: e.tensor_mul(rstd_r, mu_r, mu_r),
                             waits=[t_mu], inc="vector")
                t_var = p.op("vector", lambda e: e.tensor_sub(rstd_r, ex2_r, rstd_r),
                             waits=[t_m2_, t_ex], inc="vector")
                t_lv = p.op("scalar", lambda e: e.activation(rstd_r, rstd_r, AF.Ln, bias=epsc[0:1, 0:1]),
                            waits=[t_var, t_eps], inc="scalar")
                t_rs = p.op("scalar", lambda e: e.activation(rstd_r, rstd_r, AF.Exp, scale=-0.5),
                            waits=[t_lv], inc="scalar")
                t_mr_ = p.op("vector", lambda e: e.tensor_mul(mur_r, mu_r, rstd_r),
                             waits=[t_rs], inc="vector")
                t_b1m = p.op("tensor", lambda e: e.matmul(
                    psD[:, 0:C], onesrow[:], rstd_r, start=True, stop=True),
                    waits=[t_rs, tL["or"], blast.get("D")], inc="tensor")
                t_b2m = p.op("tensor", lambda e: e.matmul(
                    psD[:, C:2 * C], onesrow[:], mur_r, start=True, stop=True),
                    waits=[t_mr_], inc="tensor")
                t_bc1 = p.op("vector", lambda e: e.tensor_copy(bct[:, 0], psD[:, 0:C]),
                             waits=[t_b1m, t_b2m], inc="vector")
                t_bc2 = p.op("vector", lambda e: e.tensor_copy(bct[:, 1], psD[:, C:2 * C]),
                             waits=[t_b2m], inc="vector")
                blast["D"] = t_bc2
                x1v = x1.ap()[:, :, c * C:(c + 1) * C]
                x1bv = x1bf.ap()[:, :, c * C:(c + 1) * C]
                t_x1a = p.op("vector", lambda e, x1v=x1v: e.tensor_tensor(
                    x1v, sA, bct.ap()[:, 0:1, :].broadcast_to([128, JD, C]), ALU.mult),
                    waits=[t_bc1, last_x1_use], inc="vector")
                t_x1b = p.op("vector", lambda e, x1v=x1v: e.tensor_tensor(
                    x1v, x1v, bct.ap()[:, 1:2, :].broadcast_to([128, JD, C]), ALU.subtract),
                    waits=[t_bc2, t_x1a], inc="vector")
                t_x1c = p.op("vector", lambda e, x1bv=x1bv, x1v=x1v: e.tensor_copy(x1bv, x1v),
                             waits=[t_x1b, last_x1bf_use], inc="vector")
                t_x1_ready, t_x1bf_ready = t_x1b, t_x1c

            tapout("x1", x1.ap(), t_x1_ready)

            glg = None
            for j in range(JD):
                glg = p.op("tensor", lambda e, j=j: e.matmul(
                    psD[0:E, 0:TQ], wqs[:, j, :], x1.ap()[:, j, :],
                    start=(j == 0), stop=(j == JD - 1)),
                    waits=([t_x1_ready, tL["wq"], blast.get("D")] if j == 0 else ()),
                    inc=("tensor" if j == JD - 1 else None))
            t_lg = p.op("vector", lambda e: e.tensor_scalar(
                logits_sb[:], psD[0:E, 0:TQ], brs[:, 0:1], None, ALU.add),
                waits=[glg, tL["br"]], inc="vector")
            blast["D"] = t_lg
            tapout("logits", logits_sb.ap(), t_lg)
            t_tr = None
            for tt in range(NT):
                t_tr = p.op("tensor", lambda e, tt=tt: e.transpose(
                    psC[:, tt * E:(tt + 1) * E], logits_sb[:, tt * 128:(tt + 1) * 128],
                    id128[0:E, 0:E]),
                    waits=([t_lg, tL["id"], blast.get("C")] if tt == 0 else [t_lg]),
                    inc="tensor")
            t_lgT = p.op("vector", lambda e: e.tensor_copy(
                lgT[:], psC[:, 0:NT * E].rearrange("p (t e) -> p t e", t=NT)),
                waits=[t_tr], inc="vector")
            blast["C"] = t_lgT
            t_rwT = []
            for tt in range(NT):
                lgv = lgT.ap()[:, tt, :]
                ind1 = bct.ap()[:, 0, 0:E]
                mskd = bct.ap()[:, 1, 0:E]
                t_m1 = p.op("vector", lambda e, lgv=lgv: e.reduce_max(
                    srow[:, 0:1], lgv, axis=AX.X), waits=[t_lgT], inc="vector")
                t_i1 = p.op("vector", lambda e, lgv=lgv, ind1=ind1: e.tensor_scalar(
                    ind1, lgv, srow[:, 0:1], None, ALU.is_ge), waits=[t_m1], inc="vector")
                t_mk_ = p.op("vector", lambda e, lgv=lgv, ind1=ind1, mskd=mskd: e.scalar_tensor_tensor(
                    mskd, ind1, NEG_BIG, lgv, ALU.mult, ALU.add), waits=[t_i1], inc="vector")
                t_m2 = p.op("vector", lambda e, mskd=mskd: e.reduce_max(
                    srow[:, 1:2], mskd, axis=AX.X), waits=[t_mk_], inc="vector")
                t_dm = p.op("vector", lambda e: e.tensor_sub(
                    srow[:, 2:3], srow[:, 0:1], srow[:, 1:2]), waits=[t_m2], inc="vector")
                t_w1 = p.op("scalar", lambda e: e.activation(
                    srow[:, 2:3], srow[:, 2:3], AF.Sigmoid), waits=[t_dm], inc="scalar")
                t_w2 = p.op("vector", lambda e: e.tensor_scalar(
                    srow[:, 3:4], srow[:, 2:3], 1.0, -1.0, ALU.subtract, ALU.mult),
                    waits=[t_w1], inc="vector")
                t_i2 = p.op("vector", lambda e, mskd=mskd: e.tensor_scalar(
                    mskd, mskd, srow[:, 1:2], None, ALU.is_ge), waits=[t_m2, t_w2], inc="vector")
                t_r1 = p.op("vector", lambda e, ind1=ind1: e.tensor_scalar(
                    ind1, ind1, srow[:, 2:3], None, ALU.mult), waits=[t_w1, t_mk_], inc="vector")
                t_r2 = p.op("vector", lambda e, lgv=lgv, ind1=ind1, mskd=mskd: e.scalar_tensor_tensor(
                    lgv, mskd, srow[:, 3:4], ind1, ALU.mult, ALU.add),
                    waits=[t_r1, t_i2], inc="vector")
                t_rwT.append(t_r2)
            t_tb = None
            for tt in range(NT):
                t_tb = p.op("tensor", lambda e, tt=tt: e.transpose(
                    psC[0:E, tt * 128:(tt + 1) * 128], lgT.ap()[:, tt, :], id128[:]),
                    waits=([t_rwT[tt], blast.get("C")] if tt == 0 else [t_rwT[tt]]),
                    inc="tensor")
            t_rwf = p.op("vector", lambda e: e.tensor_copy(rwfm[:], psC[0:E, 0:TQ]),
                         waits=[t_tb], inc="vector")
            blast["C"] = t_rwf
            tapout("rw", rwfm.ap(), t_rwf)

            arena_gate = t_x1bf_ready
            if q == 1:
                cache_done[0] = ("gwc", p.cnt["gwc"])
            t_w1dma = {}; t_w2dma = {}
            w1_buf_of = {}; w2_buf_of = {}
            nw1 = [0]; nw2 = [0]

            def ensure_w1(e_, s_):
                key = (e_, s_)
                if key not in t_w1dma:
                    buf = nw1[0] % 2; nw1[0] += 1
                    sem = "gw1a" if buf == 0 else "gw1b"
                    if q == 0:
                        tok = p.dma("gpsimd", lambda en, e_=e_, s_=s_, buf=buf: en.dma_start(
                            w1s[buf], w1H.ap()[e_, s_]),
                            waits=list(w1_war[buf]) + [arena_gate], inc=sem)
                        cw = p.dma("sync", lambda en, e_=e_, s_=s_, buf=buf: en.dma_start(
                            w1C.ap()[e_, s_], w1s[buf]), waits=[tok], inc="gwc")
                        w1_war[buf] = [cw]
                    else:
                        tok = p.dma("gpsimd", lambda en, e_=e_, s_=s_, buf=buf: en.dma_start(
                            w1s[buf], w1C.ap()[e_, s_]),
                            waits=list(w1_war[buf]) + [arena_gate, cache_done[0]],
                            inc=("cr1a" if buf == 0 else "cr1b"))
                        w1_war[buf] = []
                    t_w1dma[key] = tok
                    w1_buf_of[key] = buf
                return t_w1dma[key], w1_buf_of[key]

            def ensure_w2(e_, dc_):
                key = (e_, dc_)
                if key not in t_w2dma:
                    buf = nw2[0] % 2; nw2[0] += 1
                    sem = "gw2a" if buf == 0 else "gw2b"
                    if q == 0:
                        tok = p.dma("gpsimd", lambda en, e_=e_, dc_=dc_, buf=buf: en.dma_start(
                            w2s[buf], w2H.ap()[e_, dc_]),
                            waits=list(w2_war[buf]) + [arena_gate], inc=sem)
                        cw = p.dma("sync", lambda en, e_=e_, dc_=dc_, buf=buf: en.dma_start(
                            w2C.ap()[e_, dc_], w2s[buf]), waits=[tok], inc="gwc")
                        w2_war[buf] = [cw]
                    else:
                        tok = p.dma("gpsimd", lambda en, e_=e_, dc_=dc_, buf=buf: en.dma_start(
                            w2s[buf], w2C.ap()[e_, dc_]),
                            waits=list(w2_war[buf]) + [arena_gate, cache_done[0]],
                            inc=("cr2a" if buf == 0 else "cr2b"))
                        w2_war[buf] = []
                    t_w2dma[key] = tok
                    w2_buf_of[key] = buf
                return t_w2dma[key], w2_buf_of[key]

            t_rwbc_prev = None
            t_hev = None
            t_tg = None
            for e_ in range(E):
                for s_ in range(8):
                    tok_dma, buf = ensure_w1(e_, s_)
                    for fc in range(4):
                        jf = s_ * 4 + fc
                        bk = ["A", "B", "C", "D", "R0", "R1", "R2", "R3"][jf % 8]
                        grp = None
                        for jd in range(JD):
                            grp = p.op("tensor", lambda e, jd=jd, fc=fc, buf=buf, bk=bk: e.matmul(
                                bank[bk][:, 0:TQ], w1s[buf][:, jd, fc * 128:(fc + 1) * 128],
                                x1bf.ap()[:, jd, :], start=(jd == 0), stop=(jd == JD - 1)),
                                waits=([tok_dma, t_x1bf_ready, blast.get(bk)]
                                       + ([last_rps_use] if bk[0] == "R" else [])
                                       if jd == 0 else ()),
                                inc=("tensor" if jd == JD - 1 else None))
                        t_hev = p.op("scalar", lambda e, jf=jf, e_=e_, bk=bk: e.activation(
                            hsb[:, jf, :], bank[bk][:, 0:TQ], AF.Relu,
                            bias=b1s.ap()[:, e_:e_ + 1, jf:jf + 1].rearrange("p a b -> p (a b)")),
                            waits=[grp, tL["b1"]], inc="scalar")
                        blast[bk] = t_hev
                        w1_war[buf].append(grp)
                    nxt = (e_, s_ + 1) if s_ + 1 < 8 else ((e_ + 1, 0) if e_ + 1 < E else None)
                    if nxt is not None:
                        ensure_w1(*nxt)
                if e_ == 0:
                    tapout("h0", hsb, t_hev)
                t_h_done = t_hev
                t_rbm = p.op("tensor", lambda e, e_=e_: e.matmul(
                    psD[:, 0:TQ], sels[:, e_, :], rwfm[:], start=True, stop=True),
                    waits=[t_rwf, tL["sel"], blast.get("D")], inc="tensor")
                t_rwbc = p.op("vector", lambda e: e.tensor_copy(rwbc[:], psD[:, 0:TQ]),
                              waits=[t_rbm, t_rwbc_prev], inc="vector")
                blast["D"] = t_rwbc
                for dc in range(8):
                    tok_dma, buf = ensure_w2(e_, dc)
                    bk = ["R0", "R1", "R2", "R3", "A", "B", "C", "D"][dc % 8]
                    grp = None
                    for jf in range(JF):
                        grp = p.op("tensor", lambda e, jf=jf, buf=buf, bk=bk: e.matmul(
                            bank[bk][:, 0:TQ], w2s[buf][:, jf, :], hsb[:, jf, :],
                            start=(jf == 0), stop=(jf == JF - 1)),
                            waits=([tok_dma, blast.get(bk)] if jf == 0
                                   else ([t_h_done] if jf == 8 else ())),
                            inc=("tensor" if jf == JF - 1 else None))
                    w2_war[buf].append(grp)
                    nxt = (e_, dc + 1) if dc + 1 < 8 else ((e_ + 1, 0) if e_ + 1 < E else None)
                    if nxt is not None:
                        ensure_w2(*nxt)
                    t_ot = p.op("vector", lambda e, bk=bk, e_=e_, dc=dc: e.scalar_tensor_tensor(
                        otmp[:], bank[bk][:, 0:TQ],
                        b2s.ap()[:, e_:e_ + 1, dc:dc + 1].rearrange("p a b -> p (a b)"),
                        rwbc[:], ALU.add, ALU.mult),
                        waits=[grp, t_rwbc, tL["b2"], t_tg], inc="vector")
                    tg = tgt2.ap()[:, dc, :]
                    if e_ == 0:
                        t_tg = p.op("vector", lambda e, tg=tg: e.tensor_copy(tg, otmp[:]),
                                    waits=[t_ot, last_tgt2_use], inc="vector")
                    else:
                        t_tg = p.op("vector", lambda e, tg=tg: e.tensor_add(tg, tg, otmp[:]),
                                    waits=[t_ot], inc="vector")
                    blast[bk] = t_tg
                t_rwbc_prev = t_tg
            t_tgt2_done = t_tg
            last_x1bf_use = t_hev
            tapout("tgt2", tgt2.ap(), t_tgt2_done)

            t_z2 = p.op("vector", lambda e: e.tensor_add(tgt2.ap(), x1.ap(), tgt2.ap()),
                        waits=[t_tgt2_done, t_x1_ready], inc="vector")

            def ln_pass(src_ap, t_src, apply_gb):
                zbf_t = hsb[:, 0:JD, :]
                zsq_t = hsb[:, JD:2 * JD, :]
                t_zb = p.op("vector", lambda e: e.tensor_copy(zbf_t, src_ap),
                            waits=[t_src], inc="vector")
                t_zq = p.op("scalar", lambda e: e.activation(zsq_t, src_ap, AF.Square),
                            waits=[t_src], inc="scalar")
                g1 = None
                for j in range(JD):
                    g1 = p.op("tensor", lambda e, j=j: e.matmul(
                        psA[0:1, 0:TQ], onescolb[:], zbf_t[:, j, :],
                        start=(j == 0), stop=(j == JD - 1)),
                        waits=([t_zb, blast.get("A")] if j == 0 else ()),
                        inc=("tensor" if j == JD - 1 else None))
                g2 = None
                for j in range(JD):
                    g2 = p.op("tensor", lambda e, j=j: e.matmul(
                        psB[0:1, 0:TQ], onescolb[:], zsq_t[:, j, :],
                        start=(j == 0), stop=(j == JD - 1)),
                        waits=([t_zq, blast.get("B")] if j == 0 else ()),
                        inc=("tensor" if j == JD - 1 else None))
                rv2 = rows.ap()
                mu_r = rv2[:, 0:TQ]
                ex_r = rv2[:, TQ:2 * TQ]
                sq_r = logits_sb.ap()[0:1, 0:TQ]
                t_mu = p.op("vector", lambda e: e.tensor_scalar_mul(mu_r, psA[0:1, 0:TQ], 1.0 / D),
                            waits=[g1], inc="vector")
                blast["A"] = t_mu
                t_ex = p.op("vector", lambda e: e.tensor_scalar_mul(ex_r, psB[0:1, 0:TQ], 1.0 / D),
                            waits=[g2], inc="vector")
                blast["B"] = t_ex
                t_q1 = p.op("vector", lambda e: e.tensor_mul(sq_r, mu_r, mu_r),
                            waits=[t_mu], inc="vector")
                t_q2 = p.op("vector", lambda e: e.tensor_sub(ex_r, ex_r, sq_r),
                            waits=[t_q1, t_ex], inc="vector")
                t_lv = p.op("scalar", lambda e: e.activation(ex_r, ex_r, AF.Ln, bias=epsc[0:1, 0:1]),
                            waits=[t_q2, t_eps], inc="scalar")
                t_rs = p.op("scalar", lambda e: e.activation(ex_r, ex_r, AF.Exp, scale=-0.5),
                            waits=[t_lv], inc="scalar")
                t_mr2 = p.op("vector", lambda e: e.tensor_mul(mu_r, mu_r, ex_r),
                             waits=[t_rs], inc="vector")
                t_bm1 = p.op("tensor", lambda e: e.matmul(
                    psC[:, 0:TQ], onesrow[:], ex_r, start=True, stop=True),
                    waits=[t_rs, blast.get("C")], inc="tensor")
                t_bm2 = p.op("tensor", lambda e: e.matmul(
                    psD[:, 0:TQ], onesrow[:], mu_r, start=True, stop=True),
                    waits=[t_mr2, blast.get("D")], inc="tensor")
                t_bc1 = p.op("vector", lambda e: e.tensor_copy(bcq[:, 0], psC[:, 0:TQ]),
                             waits=[t_bm1], inc="vector")
                blast["C"] = t_bc1
                t_bc2 = p.op("vector", lambda e: e.tensor_copy(bcq[:, 1], psD[:, 0:TQ]),
                             waits=[t_bm2], inc="vector")
                blast["D"] = t_bc2
                t_a1 = p.op("vector", lambda e: e.tensor_tensor(
                    src_ap, src_ap, bcq.ap()[:, 0:1, :].broadcast_to([128, JD, TQ]), ALU.mult),
                    waits=[t_bc1, t_zb, t_zq], inc="vector")
                t_a2 = p.op("vector", lambda e: e.tensor_tensor(
                    src_ap, src_ap, bcq.ap()[:, 1:2, :].broadcast_to([128, JD, TQ]), ALU.subtract),
                    waits=[t_bc2, t_a1], inc="vector")
                t_last = t_a2
                if apply_gb:
                    for j in range(JD):
                        t_last = p.op("vector", lambda e, j=j: e.tensor_scalar(
                            j2(src_ap, j), j2(src_ap, j), vcol(V_GAM, j), vcol(V_BET, j),
                            ALU.mult, ALU.add), waits=[t_a2], inc="vector")
                return t_last, g1, g2

            t_ln1, _, _ = ln_pass(tgt2.ap(), t_z2, apply_gb=True)
            t_z3 = p.op("vector", lambda e: e.tensor_add(tgt2.ap(), x1.ap(), tgt2.ap()),
                        waits=[t_ln1], inc="vector")
            last_x1_use = t_z3
            t_ln2, t_zb2, t_zq2 = ln_pass(tgt2.ap(), t_z3, apply_gb=False)
            t_out = p.dma("sync", lambda e, q=q: e.dma_start(
                outH.ap()[:, :, q * TQ:(q + 1) * TQ], tgt2.ap()),
                waits=[t_ln2], inc="out")
            last_tgt2_use = t_out
            tok_moe_done = t_zb2
            tok_moe_done2 = t_zq2

        p.op("sync", None, waits=[(s, p.cnt[s]) for s in P.DMA_SEMS if p.cnt[s]])

        p.emit(blk)
    return nc, tap_t



def _prep_shared(inp):
    f32 = np.float32
    def A(k):
        return np.asarray(inp[k], f32)
    def projH(Wm):
        return np.ascontiguousarray(Wm.T.reshape(JD, 128, D).transpose(1, 0, 2))
    d = {}
    d["wkH"] = projH(A("L0_attn_Wk")); d["wvH"] = projH(A("L0_attn_Wv"))
    d["wrH"] = projH(A("L0_attn_Wr")); d["woH"] = projH(A("L0_attn_Wo"))
    d["w1H"] = np.ascontiguousarray(
        A("L0_ffn_W1").reshape(E, 8, 512, JD, 128).transpose(0, 1, 4, 3, 2))
    d["w2H"] = np.ascontiguousarray(
        A("L0_ffn_W2").reshape(E, 8, 128, JF, 128).transpose(0, 1, 4, 3, 2))
    vecs = np.stack([A("L0_attn_u"), A("L0_attn_w"), A("L0_attn_tmk"),
                     A("L0_attn_tmv"), A("L0_attn_tmr"), A("L0_attn_bo"),
                     A("L0_ffn_gamma"), A("L0_ffn_beta")], 0)
    d["vecsH"] = np.ascontiguousarray(vecs.reshape(8, JD, 128).transpose(2, 0, 1))
    d["b1H"] = np.ascontiguousarray(A("L0_ffn_b1").reshape(E, JF, 128).transpose(2, 0, 1))
    d["b2H"] = np.ascontiguousarray(A("L0_ffn_b2").reshape(E, E, 128).transpose(2, 0, 1))
    d["wqH"] = np.ascontiguousarray(A("L0_ffn_Wr").T.reshape(JD, 128, E).transpose(1, 0, 2))
    d["brH"] = np.ascontiguousarray(A("L0_ffn_br").reshape(E, 1))
    d["id128H"] = np.eye(128, dtype=f32)
    d["onesH"] = np.ones((128, 1), f32)
    d["onesrowH"] = np.ones((1, 128), f32)
    sel = np.zeros((E, E, 128), f32)
    for e_ in range(E):
        sel[e_, e_, :] = 1.0
    d["selH"] = sel
    return d


def kernel(**inputs):
    x = np.asarray(inputs["L0_attn_x"], np.float32)
    B, T, _ = x.shape
    shared = _prep_shared(inputs)
    nc, _ = build(T=T)
    in_maps = []
    for b in range(B):
        m = dict(shared)
        m["xH"] = np.ascontiguousarray(x[b].T.reshape(JD, 128, T).transpose(1, 0, 2))
        in_maps.append(m)
    res = run_bass_kernel_spmd(nc, in_maps, core_ids=list(range(8)))
    out = np.empty((B, T, D), np.float32)
    for b in range(B):
        oH = np.asarray(res.results[b]["outH"])
        out[b] = oH.transpose(1, 0, 2).reshape(D, T).T
    return out
